# version 01
# baseline (speedup 1.0000x reference)
import numpy as np
import ml_dtypes
import concourse.bass as bass
import concourse.mybir as mybir
from concourse.bass_utils import run_bass_kernel_spmd
from contextlib import ExitStack

F32 = mybir.dt.float32
BF16 = mybir.dt.bfloat16
AF = mybir.ActivationFunctionType
OP = mybir.AluOpType
AX = mybir.AxisListType

MAGIC = 12582912.0
NEG_BIG = -1.0e30
D = 2048
T_CORE = 1024
T_SEQ = 2048
NH = 16
DH = 128
DQ = 682
DKV = 1024
EPS = 1e-5
I0 = 8192
I1 = 7168
NE = 8
RLORA = 64
SCALING = 0.25
HC = 16
QC6 = 6
DQ_LAST = DQ - 5 * 128
SCALE_Q = float(1.0 / np.sqrt(128.0))

DEBUG_DRAM = set()
ENG_ATTR = {"t": "tensor", "v": "vector", "s": "scalar", "g": "gpsimd", "y": "sync"}
ENG_ORDER = ("y", "g", "t", "v", "s")


class Sched:

    def __init__(self, nc):
        self.nc = nc
        self.segs = [[]]
        self.cnt = {e: 0 for e in ENG_ATTR}
        self.writer = {}
        self.readers = {}

    def op(self, eng, fn, r=(), w=(), inc=1, tag=""):
        waits = {}

        def addw(e, v):
            if v > waits.get(e, 0):
                waits[e] = v

        for b in r:
            lw = self.writer.get(b)
            if lw is not None and (lw[0] != eng or lw[0] != "t"):
                addw(*lw)
        for b in w:
            lw = self.writer.get(b)
            if lw is not None and (lw[0] != eng or lw[0] != "t"):
                addw(*lw)
            for e, v in self.readers.get(b, {}).items():
                if e != eng or e != "t":
                    addw(e, v)
        if eng == "t":
            waits.pop("t", None)
        self.cnt[eng] += inc
        ca = self.cnt[eng]
        for b in r:
            self.readers.setdefault(b, {})[eng] = ca
        for b in w:
            self.writer[b] = (eng, ca)
            self.readers[b] = {}
        self.segs[-1].append((eng, fn, inc, waits, tag))

    def wait_all(self):
        for e in ENG_ORDER:
            if self.cnt[e] > 0:
                self.segs[-1].append((e, None, 0, {e: self.cnt[e]}, "drain"))

    def barrier(self, drain=True):
        if drain:
            self.wait_all()
        self.segs.append([])

    def emit(self, sems):
        for seg in self.segs:
            if not seg:
                continue
            engines = []
            for o in seg:
                if o[0] not in engines:
                    engines.append(o[0])
            with self.nc.Block() as blk:
                for e in ENG_ORDER:
                    if e not in engines:
                        continue
                    ops_e = [o for o in seg if o[0] == e]

                    def section(E, ops_e=ops_e, e=e):
                        for (_, fn, inc, waits, tag) in ops_e:
                            for we in ENG_ORDER:
                                if we in waits:
                                    E.wait_ge(sems[we], waits[we])
                            if fn is not None:
                                inst = fn(E)
                                if inc:
                                    inst.then_inc(sems[e], inc)

                    getattr(blk, ENG_ATTR[e])(section)


def _ct(a):
    return np.ascontiguousarray(a)


def _qperm():
    p = []
    for h in range(NH):
        b = h * DH
        p += list(range(b + 64, b + 96)) + list(range(b + 96, b + 128)) \
            + list(range(b, b + 64))
    return np.array(p)


def _kvperm_pad(w):
    out = np.zeros((w.shape[0], 4096), dtype=w.dtype)
    for g in range(NH // 2):
        a, b = 2 * g, 2 * g + 1
        o = g * 512
        out[:, o + 64:o + 128] = w[:, a * 192:a * 192 + 64]
        out[:, o + 192:o + 256] = w[:, b * 192:b * 192 + 64]
        out[:, o + 256:o + 384] = w[:, a * 192 + 64:a * 192 + 192]
        out[:, o + 384:o + 512] = w[:, b * 192 + 64:b * 192 + 192]
    return out


def host_prepare(inp):
    sh = {}
    sh["wdqT"] = _ct(inp["L0_attn_W_d_q"].T)
    sh["bdq"] = _ct(inp["L0_attn_b_d_q"].reshape(1, DQ))
    sh["qg"] = _ct(inp["L0_attn_q_norm_g"].reshape(1, DQ))
    sh["qb"] = _ct(inp["L0_attn_q_norm_b"].reshape(1, DQ))
    qp = _qperm()
    sh["wuqT"] = _ct(inp["L0_attn_W_u_q"].T[:, qp])
    sh["buq"] = _ct(inp["L0_attn_b_u_q"][qp].reshape(1, D))
    sh["wdkvT"] = _ct(inp["L0_attn_W_d_kv"].T)
    sh["bdkv"] = _ct(inp["L0_attn_b_d_kv"].reshape(1, 1088))
    sh["kvg"] = _ct(inp["L0_attn_kv_norm_g"].reshape(1, DKV))
    sh["kvb"] = _ct(inp["L0_attn_kv_norm_b"].reshape(1, DKV))
    sh["wukvT"] = _ct(_kvperm_pad(np.asarray(inp["L0_attn_W_u_kv"]).T))
    sh["bukv"] = _ct(_kvperm_pad(
        np.asarray(inp["L0_attn_b_u_kv"]).reshape(1, 3072)))
    sh["woT"] = _ct(inp["L0_attn_W_o"].T)
    sh["bo"] = _ct(inp["L0_attn_b_o"].reshape(1, D))
    sh["wgT"] = _ct(inp["L0_ffn_w_gate"].T)
    sh["wuT"] = _ct(inp["L0_ffn_w_up"].T)
    sh["wdT"] = _ct(inp["L0_ffn_w_down"].T)
    sh["wqkv"] = _ct(inp["L1_attn_w_qkv"])
    sh["wout"] = _ct(inp["L1_attn_w_out"])
    sh["decay"] = _ct(inp["L1_attn_decay_param"].reshape(1, NH))
    sh["wealth"] = _ct(inp["L1_ffn_expert_wealth"].reshape(1, NE))
    sh["confWT"] = _ct(inp["L1_ffn_conf_W"].T)
    sh["confb"] = _ct(inp["L1_ffn_conf_b"].reshape(1, NE))
    sh["bgate"] = _ct(inp["L1_ffn_base_gate"])
    sh["bup"] = _ct(inp["L1_ffn_base_up"])
    sh["bdown"] = _ct(inp["L1_ffn_base_down"])
    sh["gateA"] = _ct(inp["L1_ffn_gate_A"])
    sh["gateB"] = _ct(inp["L1_ffn_gate_B"])
    sh["upA"] = _ct(inp["L1_ffn_up_A"])
    sh["upB"] = _ct(inp["L1_ffn_up_B"])
    sh["downA"] = _ct(inp["L1_ffn_down_A"])
    sh["downB"] = _ct(inp["L1_ffn_down_B"])

    sh["ident_f"] = np.eye(128, dtype=np.float32)
    j = np.arange(32, dtype=np.float32)
    inv_freq = (1.0 / 10000.0) ** (2.0 * j / 64.0)
    hh = np.arange(NH, dtype=np.float32)
    freqs = hh[None, :] * inv_freq[:, None]
    cs = np.cos(freqs).astype(np.float32)
    sn = np.sin(freqs).astype(np.float32)
    sh["cos_t"] = _ct(np.concatenate([cs, cs], axis=0))
    sh["sin_t"] = _ct(np.concatenate([sn, sn], axis=0))
    s = np.arange(128, dtype=np.float32)
    sh["iotaE"] = _ct((s[None, :] - s[:, None]).astype(np.float32))
    sh["sc_dvec"] = _ct((s + 1.0).reshape(1, 128))
    dg = np.zeros((1, 1024), dtype=np.float32)
    for c in range(8):
        dg[0, c * 128:(c + 1) * 128] = 128.0 * c + s + 1.0
    sh["sc_dvecg"] = dg
    ud = np.zeros((128, 8), dtype=np.float32)
    for dl in range(1, 9):
        ud[:, dl - 1] = 128.0 * dl - 1.0 - s
    sh["sc_udelta"] = ud

    tri = np.where(np.arange(128)[None, :] < np.arange(128)[:, None],
                   NEG_BIG, 0.0).astype(np.float32)

    cores = []
    x = np.asarray(inp["L0_attn_x"]).reshape(4 * T_SEQ, D)
    for c in range(8):
        b, half = c // 2, c % 2
        pc = {}
        pc["x_own"] = _ct(x[b * T_SEQ + half * T_CORE:
                            b * T_SEQ + (half + 1) * T_CORE])
        pc["x_seq"] = _ct(x[b * T_SEQ:(b + 1) * T_SEQ])
        pc["halfmask"] = np.array([[float(half)]], dtype=np.float32)
        st = np.zeros((16, 128, 128), dtype=np.float32)
        mr = np.zeros((16, 1024), dtype=np.float32)
        for kc in range(16):
            tk0 = kc * 128
            d0 = tk0 - half * T_CORE
            if 0 <= d0 < T_CORE and d0 == (kc % 8) * 128:
                st[kc] = tri
            elif half == 0 and kc >= 8:
                st[kc] = tri
            tq = half * T_CORE + np.arange(T_CORE)
            mr[kc] = np.where(tq < tk0, 1.0e30, 0.0)
        pc["tristack"] = st.reshape(16 * 128, 128)
        pc["maskrows"] = mr
        cores.append(pc)
    return sh, cores


def apn(handle, poff, pcnt, foff, dims):
    fsz = 1
    for d in handle.shape[1:]:
        fsz *= d
    pat = [[fsz, pcnt]] + [[st, ct] for (st, ct) in dims]
    return bass.AP(handle, poff * fsz + foff, pat)


class Builder:
    def __init__(self, phases, first, last):
        self.phases = phases
        self.first = first
        self.last = last
        self.nc = bass.Bass("TRN2", target_bir_lowering=False)
        self.S = Sched(self.nc)
        self.ctx = ExitStack()
        self.t = {}
        self.dram = {}
        self.in_names = []

    def din(self, name, shape, dtype=F32):
        if name not in self.dram:
            self.dram[name] = self.nc.dram_tensor("in_" + name, list(shape),
                                                  dtype, kind="ExternalInput")
            self.in_names.append(name)
        return self.dram[name]

    def dout(self, name, shape, dtype=F32):
        self.dram[name] = self.nc.dram_tensor("out_" + name, list(shape), dtype,
                                              kind="ExternalOutput")
        return self.dram[name]

    def dint(self, name, shape, dtype=F32):
        if name not in self.dram:
            if name in DEBUG_DRAM:
                self.dram[name] = self.nc.dram_tensor(
                    "out_" + name, list(shape), dtype, kind="ExternalOutput")
            else:
                self.dram[name] = self.nc.dram_tensor(name, list(shape), dtype)
        return self.dram[name]

    def _side(self, stack):
        if stack is None or stack is self.ctx or stack is getattr(
                self, "phase_stack", None):
            return "left"
        return "right"

    def sb(self, name, shape, dtype=F32, stack=None):
        side = self._side(stack)
        h = (stack if stack is not None else self.ctx).enter_context(
            self.nc.sbuf_tensor(name, list(shape), dtype, side=side))
        self.t[name] = h
        return h

    def ps(self, name, shape, dtype=F32, stack=None):
        side = self._side(stack)
        h = (stack if stack is not None else self.ctx).enter_context(
            self.nc.psum_tensor(name, list(shape), dtype, side=side))
        self.t[name] = h
        return h

    def dma(self, q, dst, src, r, w, tag=""):
        self.S.op(q, lambda E, d=dst, s=src: E.dma_start(out=d, in_=s),
                  r=r, w=w, inc=16, tag=tag)

    def mm(self, pairs, r, w, tag=""):
        def fn(E, pairs=pairs):
            inst = None
            for (l, rr, o, st_, sp_) in pairs:
                inst = E.matmul(o, l, rr, start=st_, stop=sp_,
                                skip_group_check=True)
            return inst
        self.S.op("t", fn, r=r, w=w, tag=tag)

    def trans(self, out_ps, in_ap, ident, r, w, tag=""):
        self.S.op("t", lambda E, o=out_ps, i=in_ap, d=ident: E.transpose(o, i, d),
                  r=r, w=w, tag=tag)

    def vop(self, fn, r, w, tag="", eng="v"):
        self.S.op(eng, fn, r=r, w=w, tag=tag)

    def act(self, fn, r, w, tag=""):
        self.S.op("s", fn, r=r, w=w, tag=tag)

    def emit_consts(self, st):
        sb, din = self.sb, self.din
        self.ident_f = sb("ident_f", [128, 128], F32, st)
        self.ident_b = sb("ident_b", [128, 128], BF16, st)
        self.ones_col_b = sb("ones_col_b", [128, 1], BF16, st)
        self.ones_col_f = sb("ones_col_f", [128, 1], F32, st)
        self.ones_row_f = sb("ones_row_f", [1, 128], F32, st)
        self.negones_row_b = sb("negones_row_b", [1, 128], BF16, st)
        self.dma("y", self.ident_f[:, :], din("ident_f", [128, 128])[:, :],
                 r=(), w=("ident_f",))
        self.dma("g", self.ident_b[:, :], din("ident_f", [128, 128])[:, :],
                 r=(), w=("ident_b",))
        for tl, v, k in ((self.ones_col_b, 1.0, "ones_col_b"),
                         (self.ones_col_f, 1.0, "ones_col_f"),
                         (self.ones_row_f, 1.0, "ones_row_f"),
                         (self.negones_row_b, -1.0, "negones_row_b")):
            self.vop(lambda E, t=tl, vv=v: E.memset(t[:, :], vv), r=(), w=(k,))

    def load_row(self, st, name, dname, n, dtype=F32):
        tl = self.sb(name, [1, n], dtype, st)
        dh = self.din(dname, [1, n])
        self.dma("y" if dtype == F32 else "g", tl[:, :], dh[:, :], r=(), w=(name,))
        return tl

    def bcast_row(self, st, name, row, n, rkey, psname=None, dtype=F32):
        psname = psname or self.psbc_name
        out = self.sb(name, [128, n], dtype, st)
        pst = self.t[psname]
        for j0 in range(0, n, 512):
            jn = min(512, n - j0)
            self.mm([(self.ones_row_f[:, :], row[0:1, j0:j0 + jn],
                      pst[:, 0:jn], True, True)],
                    r=("ones_row_f", rkey), w=(psname,), tag="bc")
            self.vop(lambda E, o=out, ps=pst, j0=j0, jn=jn:
                     E.tensor_copy(o[:, j0:j0 + jn], ps[:, 0:jn]),
                     r=(psname,), w=(name,))
        return out

    def row_to_col(self, row_ap, n, col_ap, rkey, wkey, psname=None):
        psname = psname or self.psbc_name
        pst = self.t[psname]
        self.trans(pst[0:n, 0:1], row_ap, self.ident_f[0:1, 0:1],
                   r=(rkey, "ident_f"), w=(psname,))
        self.vop(lambda E, o=col_ap, ps=pst, n=n: E.tensor_copy(o, ps[0:n, 0:1]),
                 r=(psname,), w=(wkey,))


def emit_ln_tm(B, x_ap, n, out_ap, sq_ap, sq_key, mcol, msq, rcol, skey,
               rkeys, wkeys):
    B.act(lambda E, x=x_ap, sq=sq_ap: E.activation(sq, x, AF.Square),
          r=rkeys, w=(sq_key,))
    B.vop(lambda E, x=x_ap, m=mcol: E.reduce_sum(m, x, axis=AX.X),
          r=rkeys, w=(skey + "m",))
    B.vop(lambda E, sq=sq_ap, rr=rcol: E.reduce_sum(rr, sq, axis=AX.X),
          r=(sq_key,), w=(skey + "r",))
    inv = 1.0 / n
    B.vop(lambda E, m=mcol: E.tensor_single_scalar(m, m, inv, OP.mult),
          r=(skey + "m",), w=(skey + "m",))
    B.vop(lambda E, rr=rcol: E.tensor_single_scalar(rr, rr, inv, OP.mult),
          r=(skey + "r",), w=(skey + "r",))
    B.vop(lambda E, m=mcol, q=msq: E.tensor_mul(q, m, m),
          r=(skey + "m",), w=(skey + "q",))
    B.vop(lambda E, rr=rcol, q=msq: E.tensor_scalar(rr, rr, q, EPS,
                                                    OP.subtract, OP.add),
          r=(skey + "r", skey + "q"), w=(skey + "r",))
    B.act(lambda E, rr=rcol: E.activation(rr, rr, AF.Sqrt),
          r=(skey + "r",), w=(skey + "r",))
    B.vop(lambda E, rr=rcol: E.reciprocal(rr, rr),
          r=(skey + "r",), w=(skey + "r",))
    B.vop(lambda E, x=x_ap, o=out_ap, m=mcol, rr=rcol:
          E.tensor_scalar(o, x, m, rr, OP.subtract, OP.mult),
          r=rkeys + (skey + "m", skey + "r"), w=wkeys)


def phase_l0a(B):
    S = B.S
    st = B.ctx.enter_context(ExitStack())
    B.phase_stack = st

    kvn_fm = B.sb("kvn_fm", [128, 8 * T_SEQ], BF16, st)
    krT = B.sb("krT", [128, T_SEQ], BF16, st)
    cqn_fm = B.sb("cqn_fm", [128, QC6 * 1024], BF16, st)
    x_b = B.dint("x_bounce", [D, T_CORE])
    B.emit_consts(st)
    st_xn = ExitStack()
    xn = B.sb("xn", [128, HC * 1024], BF16, st_xn)

    st2 = ExitStack()
    psbc = B.ps("psbc", [128, 512], F32, st2)
    B.psbc_name = "psbc"
    bdkv_r = B.load_row(st2, "bdkv_r", "bdkv", 1088)
    kvg_r = B.load_row(st2, "kvg_r", "kvg", DKV)
    kvb_r = B.load_row(st2, "kvb_r", "kvb", DKV)
    bdkv_b = B.bcast_row(st2, "bdkv_b", bdkv_r[:, :], 1088, "bdkv_r")
    kvg_b = B.bcast_row(st2, "kvg_b", kvg_r[:, :], DKV, "kvg_r")
    kvb_b = B.bcast_row(st2, "kvb_b", kvb_r[:, :], DKV, "kvb_r")
    wdkv = B.sb("wdkv", [128, HC * 1088], BF16, st2)
    wd = B.din("wdkvT", [D, 1088])
    for hc in range(HC):
        B.dma("g", wdkv[:, hc * 1088:(hc + 1) * 1088],
              wd[hc * 128:(hc + 1) * 128, :], r=(), w=(f"wdkv{hc}",))
    xs_tm = [B.sb(f"xs_tm{i}", [128, D], F32, st2) for i in range(2)]
    sq2 = B.sb("sq2", [128, D], F32, st2)
    xnk_tm = B.sb("xnk_tm", [128, D], BF16, st2)
    xnk_fm = B.sb("xnk_fm", [128, HC * 128], BF16, st2)
    ckv_tm = B.sb("ckv_tm", [128, 1088], F32, st2)
    kvn_tm = B.sb("kvn_tm", [128, DKV], BF16, st2)
    tmp_tm = B.sb("tmp_tm", [128, DKV], F32, st2)
    m2 = B.sb("m2", [128, 1], F32, st2)
    q2c = B.sb("q2c", [128, 1], F32, st2)
    r2 = B.sb("r2", [128, 1], F32, st2)
    psK = [B.ps(f"psK{i}", [128, 512], F32, st2) for i in range(3)]
    psQ = B.ps("psQ", [128, 512], BF16, st2)

    xseq = B.din("x_seq", [T_SEQ, D])
    wdkv_keys = tuple(f"wdkv{h}" for h in range(HC))
    for tc in range(16):
        xb = f"xs_tm{tc % 2}"
        B.dma("y", xs_tm[tc % 2][:, :], xseq[tc * 128:(tc + 1) * 128, :],
              r=(), w=(xb,))
        emit_ln_tm(B, xs_tm[tc % 2][:, :], D, xnk_tm[:, :], sq2[:, :], "sq2",
                   m2[:, :], q2c[:, :], r2[:, :], "s2", (xb,), ("xnk_tm",))
        for h0 in range(0, 16, 4):
            for h in range(h0, h0 + 4):
                B.trans(psQ[:, (h - h0) * 128:(h - h0 + 1) * 128],
                        xnk_tm[:, h * 128:(h + 1) * 128], B.ident_b[:, :],
                        r=("xnk_tm", "ident_b"), w=("psQ",))
            B.vop(lambda E, h0=h0: E.tensor_copy(
                xnk_fm[:, h0 * 128:(h0 + 4) * 128], psQ[:, 0:512]),
                r=("psQ",), w=("xnk_fm",))
        for j, (o0, on) in enumerate(((0, 512), (512, 512), (1024, 64))):
            pairs = []
            for hc in range(HC):
                pairs.append((xnk_fm[:, hc * 128:(hc + 1) * 128],
                              wdkv[:, hc * 1088 + o0: hc * 1088 + o0 + on],
                              psK[j][:, 0:on], hc == 0, hc == HC - 1))
            B.mm(pairs, r=("xnk_fm",) + wdkv_keys, w=(f"psK{j}",), tag=f"ckv{tc}")
            B.vop(lambda E, j=j, o0=o0, on=on: E.scalar_tensor_tensor(
                ckv_tm[:, o0:o0 + on], psK[j][:, 0:on], 1.0,
                bdkv_b[:, o0:o0 + on], OP.mult, OP.add),
                r=(f"psK{j}", "bdkv_b"), w=("ckv_tm",))
        emit_ln_tm(B, ckv_tm[:, 0:DKV], DKV, tmp_tm[:, :], sq2[:, 0:DKV], "sq2",
                   m2[:, :], q2c[:, :], r2[:, :], "s2", ("ckv_tm",), ("tmp_tm",))
        B.vop(lambda E: E.tensor_mul(tmp_tm[:, :], tmp_tm[:, :], kvg_b[:, :]),
              r=("tmp_tm", "kvg_b"), w=("tmp_tm",))
        B.vop(lambda E: E.tensor_tensor(kvn_tm[:, :], tmp_tm[:, :],
                                        kvb_b[:, :], OP.add),
              r=("tmp_tm", "kvb_b"), w=("kvn_tm",))
        for c0 in range(0, 8, 4):
            for cc in range(c0, c0 + 4):
                B.trans(psQ[:, (cc - c0) * 128:(cc - c0 + 1) * 128],
                        kvn_tm[:, cc * 128:(cc + 1) * 128], B.ident_b[:, :],
                        r=("kvn_tm", "ident_b"), w=("psQ",))
            B.vop(lambda E, c0=c0, tc=tc: E.tensor_copy(
                apn(kvn_fm, 0, 128, c0 * T_SEQ + tc * 128, [(T_SEQ, 4), (1, 128)]),
                psQ[:, 0:512]), r=("psQ",), w=("kvn_fm",))
        B.vop(lambda E: E.tensor_copy(xnk_tm[:, 0:64], ckv_tm[:, 1024:1088]),
              r=("ckv_tm",), w=("xnk_tm",))
        B.trans(psQ[0:64, 0:128], xnk_tm[:, 0:64], B.ident_b[:, :],
                r=("xnk_tm", "ident_b"), w=("psQ",))
        B.vop(lambda E, tc=tc: E.tensor_copy(
            krT[0:64, tc * 128:(tc + 1) * 128], psQ[0:64, 0:128]),
            r=("psQ",), w=("krT",))
    S.barrier()
    st2.close()

    st1 = ExitStack()
    x_tm = [B.sb(f"x_tm{i}", [128, D], F32, st1) for i in range(2)]
    sq1 = B.sb("sq1", [128, D], F32, st1)
    xn_tm = B.sb("xn_tm", [128, D], BF16, st1)
    xf_tile = B.sb("xf_tile", [128, HC * 128], F32, st1)
    m1 = B.sb("m1", [128, 1], F32, st1)
    q1c = B.sb("q1c", [128, 1], F32, st1)
    r1 = B.sb("r1", [128, 1], F32, st1)
    psF = B.ps("psF", [128, 512], F32, st1)
    psX = B.ps("psX", [128, 512], BF16, st1)
    x_src = B.din("x_own", [T_CORE, D])
    for tc in range(8):
        xb = f"x_tm{tc % 2}"
        B.dma("y", x_tm[tc % 2][:, :], x_src[tc * 128:(tc + 1) * 128, :],
              r=(), w=(xb,))
        for h0 in range(0, 16, 4):
            for h in range(h0, h0 + 4):
                B.trans(psF[:, (h - h0) * 128:(h - h0 + 1) * 128],
                        x_tm[tc % 2][:, h * 128:(h + 1) * 128], B.ident_f[:, :],
                        r=(xb, "ident_f"), w=("psF",))
            B.vop(lambda E, h0=h0: E.tensor_copy(
                xf_tile[:, h0 * 128:(h0 + 4) * 128], psF[:, 0:512]),
                r=("psF",), w=("xf_tile",))
        B.dma("y", bass.AP(x_b, tc * 128,
                           [[1024, 128], [128 * 1024, HC], [1, 128]]),
              apn(xf_tile, 0, 128, 0, [(128, HC), (1, 128)]),
              r=("xf_tile",), w=("x_b",), tag=f"xb{tc}")
        emit_ln_tm(B, x_tm[tc % 2][:, :], D, xn_tm[:, :], sq1[:, :], "sq1",
                   m1[:, :], q1c[:, :], r1[:, :], "s1", (xb,), ("xn_tm",))
        for h0 in range(0, 16, 4):
            for h in range(h0, h0 + 4):
                B.trans(psX[:, (h - h0) * 128:(h - h0 + 1) * 128],
                        xn_tm[:, h * 128:(h + 1) * 128], B.ident_b[:, :],
                        r=("xn_tm", "ident_b"), w=("psX",))
            B.vop(lambda E, h0=h0, tc=tc: E.tensor_copy(
                apn(xn, 0, 128, h0 * 1024 + tc * 128, [(1024, 4), (1, 128)]),
                psX[:, 0:512]), r=("psX",), w=("xn",))
    S.barrier()
    st1.close()

    st3 = ExitStack()
    psbc = B.ps("psbc3", [128, 512], F32, st3)
    B.psbc_name = "psbc3"
    bdq_r = B.load_row(st3, "bdq_r", "bdq", DQ)
    qg_r = B.load_row(st3, "qg_r", "qg", DQ)
    qb_r = B.load_row(st3, "qb_r", "qb", DQ)
    bdq_b = B.bcast_row(st3, "bdq_b", bdq_r[:, :], DQ, "bdq_r")
    qg_b = B.bcast_row(st3, "qg_b", qg_r[:, :], DQ, "qg_r")
    qb_b = B.bcast_row(st3, "qb_b", qb_r[:, :], DQ, "qb_r")
    wdq = B.sb("wdq", [128, HC * DQ], BF16, st3)
    wq_d = B.din("wdqT", [D, DQ])
    for hc in range(HC):
        B.dma("g", wdq[:, hc * DQ:(hc + 1) * DQ],
              wq_d[hc * 128:(hc + 1) * 128, :], r=(), w=(f"wdq{hc}",))
    cq_tm = B.sb("cq_tm", [128, DQ], F32, st3)
    cqn_tm = B.sb("cqn_tm", [128, DQ], BF16, st3)
    tmpq = B.sb("tmpq", [128, DQ], F32, st3)
    sq3 = B.sb("sq3", [128, DQ], F32, st3)
    m3 = B.sb("m3", [128, 1], F32, st3)
    q3c = B.sb("q3c", [128, 1], F32, st3)
    r3 = B.sb("r3", [128, 1], F32, st3)
    psC = [B.ps(f"psC{i}", [128, 512], F32, st3) for i in range(2)]
    psQ3 = B.ps("psQ3", [128, 512], BF16, st3)
    wdq_keys = tuple(f"wdq{h}" for h in range(HC))
    for tc in range(8):
        for j, (o0, on) in enumerate(((0, 512), (512, 170))):
            pairs = []
            for hc in range(HC):
                pairs.append((xn[:, hc * 1024 + tc * 128: hc * 1024 + (tc + 1) * 128],
                              wdq[:, hc * DQ + o0: hc * DQ + o0 + on],
                              psC[j][:, 0:on], hc == 0, hc == HC - 1))
            B.mm(pairs, r=("xn",) + wdq_keys, w=(f"psC{j}",), tag=f"cq{tc}")
            B.vop(lambda E, j=j, o0=o0, on=on: E.scalar_tensor_tensor(
                cq_tm[:, o0:o0 + on], psC[j][:, 0:on], 1.0,
                bdq_b[:, o0:o0 + on], OP.mult, OP.add),
                r=(f"psC{j}", "bdq_b"), w=("cq_tm",))
        emit_ln_tm(B, cq_tm[:, :], DQ, tmpq[:, :], sq3[:, :], "sq3",
                   m3[:, :], q3c[:, :], r3[:, :], "s3", ("cq_tm",), ("tmpq",))
        B.vop(lambda E: E.tensor_mul(tmpq[:, :], tmpq[:, :], qg_b[:, :]),
              r=("tmpq", "qg_b"), w=("tmpq",))
        B.vop(lambda E: E.tensor_tensor(cqn_tm[:, :], tmpq[:, :], qb_b[:, :],
                                        OP.add), r=("tmpq", "qb_b"),
              w=("cqn_tm",))
        for cc in range(QC6):
            cn = 128 if cc < 5 else DQ_LAST
            B.trans(psQ3[0:cn, 0:128], cqn_tm[:, cc * 128:cc * 128 + cn],
                    B.ident_b[:, :], r=("cqn_tm", "ident_b"), w=("psQ3",))
            B.vop(lambda E, cc=cc, tc=tc, cn=cn: E.tensor_copy(
                cqn_fm[0:cn, cc * 1024 + tc * 128: cc * 1024 + (tc + 1) * 128],
                psQ3[0:cn, 0:128]), r=("psQ3",), w=("cqn_fm",))
    S.barrier()
    st3.close()
    st_xn.close()

    st4 = ExitStack()
    psD = B.ps("psD", [128, 512], F32, st4)
    psE = B.ps("psE", [128, 512], F32, st4)
    B.psbc_name = "psD"
    cos_t = B.sb("cos_t", [64, NH], F32, st4)
    sin_t = B.sb("sin_t", [64, NH], F32, st4)
    B.dma("y", cos_t[:, :], B.din("cos_t", [64, NH])[:, :], r=(), w=("cos_t",))
    B.dma("y", sin_t[:, :], B.din("sin_t", [64, NH])[:, :], r=(), w=("sin_t",))
    tristack = B.sb("tristack", [128, 16 * 128], BF16, st4)
    tri_d = B.din("tristack", [16 * 128, 128])
    B.dma("g", apn(tristack, 0, 128, 0, [(128, 16), (1, 128)]),
          bass.AP(tri_d, 0, [[128, 128], [128 * 128, 16], [1, 128]]),
          r=(), w=("tristack",))
    maskrows = B.sb("maskrows", [1, 16 * 1024], BF16, st4)
    mr_d = B.din("maskrows", [16, 1024])
    B.dma("g", maskrows[0:1, :],
          bass.AP(mr_d, 0, [[16 * 1024, 1], [1, 16 * 1024]]),
          r=(), w=("maskrows",))
    bukv_c = B.sb("bukv_c", [128, 32], F32, st4)
    buq_c = B.sb("buq_c", [128, 16], F32, st4)
    st4r = ExitStack()
    bukv_r = B.load_row(st4r, "bukv_r", "bukv", 4096)
    buq_r = B.load_row(st4r, "buq_r", "buq", D)
    for jj in range(32):
        B.row_to_col(bukv_r[0:1, jj * 128:(jj + 1) * 128], 128,
                     bukv_c[:, jj:jj + 1], "bukv_r", "bukv_c")
    for jj in range(16):
        B.row_to_col(buq_r[0:1, jj * 128:(jj + 1) * 128], 128,
                     buq_c[:, jj:jj + 1], "buq_r", "buq_c")
    B.vop(lambda E: E.tensor_single_scalar(buq_c[:, :], buq_c[:, :], SCALE_Q,
                                           OP.mult), r=("buq_c",), w=("buq_c",))
    S.barrier()
    st4r.close()
    wukv = [B.sb(f"wukv{i}", [128, 8 * 512], BF16, st4) for i in range(2)]
    wuq = [B.sb(f"wuq{i}", [128, QC6 * 256], BF16, st4) for i in range(2)]
    kfull = [B.sb(f"kfull{i}", [128, T_SEQ], BF16, st4) for i in range(2)]
    scr_b = B.sb("scr_b", [128, T_SEQ], BF16, st4)
    vf32 = B.sb("vf32", [128, T_SEQ], F32, st4)
    vT = [B.sb(f"vT{i}", [128, T_SEQ], BF16, st4) for i in range(2)]
    qt = [B.sb(f"qt{i}", [128, 1024], BF16, st4) for i in range(2)]
    qraw = B.sb("qraw", [128, 1024], BF16, st4)
    Rt = B.sb("Rt", [128, 128], BF16, st4)
    nrow = B.sb("nrow", [1, T_SEQ], F32, st4)
    maxk = B.sb("maxk", [1, 1], F32, st4)
    crow = B.sb("crow", [1, 1024], BF16, st4)
    crow_f = B.sb("crow_f", [1, 1024], F32, st4)
    Pt = [B.sb(f"Pt{i}", [128, 1024], BF16, st4) for i in range(2)]
    y_raw = B.sb("y_raw", [128, 1024], F32, st4)
    den_r = B.sb("den_r", [1, 1024], F32, st4)
    y_fm = B.sb("y_fm", [128, HC * 1024], BF16, st4)
    psA = B.ps("psA", [128, 1024], F32, st4)
    psB = B.ps("psB", [128, 1024], F32, st4)
    psY = B.ps("psY", [128, 1024], F32, st4)
    wukv_d = B.din("wukvT", [DKV, 4096])
    wuq_d = B.din("wuqT", [DQ, D])

    for g in range(8):
        gb = g % 2
        wk, wqk = f"wukv{gb}", f"wuq{gb}"
        for c in range(8):
            B.dma("g", wukv[gb][:, c * 512:(c + 1) * 512],
                  wukv_d[c * 128:(c + 1) * 128, g * 512:(g + 1) * 512],
                  r=(), w=(wk,))
        for cc in range(QC6):
            cn = 128 if cc < 5 else DQ_LAST
            B.dma("g", wuq[gb][0:cn, cc * 256:(cc + 1) * 256],
                  wuq_d[cc * 128:cc * 128 + cn, g * 256:(g + 1) * 256],
                  r=(), w=(wqk,))
        for oc in range(4):
            for tb in range(4):
                pk, pt = ("psD", psD) if tb % 2 == 0 else ("psE", psE)
                pairs = []
                for c in range(8):
                    pairs.append((wukv[gb][:, c * 512 + oc * 128:
                                           c * 512 + (oc + 1) * 128],
                                  kvn_fm[:, c * T_SEQ + tb * 512:
                                         c * T_SEQ + (tb + 1) * 512],
                                  pt[:, 0:512], c == 0, c == 7))
                B.mm(pairs, r=(wk, "kvn_fm"), w=(pk,), tag=f"kv{g}_{oc}{tb}")
                if oc < 2:
                    kf, kfk = kfull[oc], f"kfull{oc}"
                    B.act(lambda E, tb=tb, g=g, oc=oc, pt=pt, kf=kf:
                          E.activation(kf[64:128, tb * 512:(tb + 1) * 512],
                                       pt[64:128, 0:512], AF.Identity,
                                       bias=bukv_c[64:128,
                                                   4 * g + oc:4 * g + oc + 1]),
                          r=(pk, "bukv_c"), w=(kfk,))
                else:
                    B.act(lambda E, tb=tb, g=g, oc=oc, pt=pt: E.activation(
                        vf32[:, tb * 512:(tb + 1) * 512], pt[:, 0:512],
                        AF.Identity,
                        bias=bukv_c[:, 4 * g + oc:4 * g + oc + 1]),
                        r=(pk, "bukv_c"), w=("vf32",))
            if oc >= 2:
                hh = oc - 2
                for c0 in range(0, 16, 4):
                    for cc in range(c0, c0 + 4):
                        B.trans(psA[:, (cc - c0) * 128:(cc - c0 + 1) * 128],
                                vf32[:, cc * 128:(cc + 1) * 128],
                                B.ident_f[:, :], r=("vf32", "ident_f"),
                                w=("psA",))
                    B.vop(lambda E, hh=hh, c0=c0: E.tensor_copy(
                        vT[hh][:, c0 * 128:(c0 + 4) * 128], psA[:, 0:512]),
                        r=("psA",), w=(f"vT{hh}",))
        for hh in range(2):
            kf, kfk = kfull[hh], f"kfull{hh}"
            B.vop(lambda E, kf=kf: E.tensor_copy(kf[0:64, :], krT[0:64, :]),
                  r=("krT",), w=(kfk,))
        for hh in range(2):
            head = 2 * g + hh
            qk = f"qt{hh}"
            pairs = []
            for j in range(2):
                for cc in range(QC6):
                    cn = 128 if cc < 5 else DQ_LAST
                    pairs.append((wuq[gb][0:cn, cc * 256 + hh * 128:
                                          cc * 256 + hh * 128 + 128],
                                  cqn_fm[0:cn, cc * 1024 + j * 512:
                                         cc * 1024 + (j + 1) * 512],
                                  psB[:, j * 512:(j + 1) * 512],
                                  cc == 0, cc == QC6 - 1))
            B.mm(pairs, r=(wqk, "cqn_fm"), w=("psB",), tag=f"q{head}")
            B.act(lambda E, head=head: E.activation(
                qraw[:, :], psB[:, 0:1024], AF.Identity,
                bias=buq_c[:, head:head + 1], scale=SCALE_Q),
                r=("psB", "buq_c"), w=("qraw",))
            B.vop(lambda E: E.memset(Rt[:, :], 0.0), r=(), w=("Rt",))
            B.vop(lambda E, head=head: E.tensor_scalar_mul(
                Rt[0:32, 0:32], B.ident_b[0:32, 0:32],
                cos_t[0:32, head:head + 1]), r=("ident_b", "cos_t"), w=("Rt",))
            B.vop(lambda E, head=head: E.tensor_scalar(
                Rt[0:32, 32:64], B.ident_b[0:32, 0:32],
                sin_t[0:32, head:head + 1], -1.0, OP.mult, OP.mult),
                r=("ident_b", "sin_t"), w=("Rt",))
            B.vop(lambda E, head=head: E.tensor_scalar_mul(
                Rt[32:64, 0:32], B.ident_b[32:64, 32:64],
                sin_t[32:64, head:head + 1]), r=("ident_b", "sin_t"), w=("Rt",))
            B.vop(lambda E, head=head: E.tensor_scalar_mul(
                Rt[32:64, 32:64], B.ident_b[32:64, 32:64],
                cos_t[32:64, head:head + 1]), r=("ident_b", "cos_t"), w=("Rt",))
            B.vop(lambda E: E.tensor_copy(Rt[64:128, 64:128],
                                          B.ident_b[64:128, 64:128]),
                  r=("ident_b",), w=("Rt",))
            B.mm([(Rt[:, :], qraw[:, 0:512], psB[:, 0:512], True, True),
                  (Rt[:, :], qraw[:, 512:1024], psB[:, 512:1024], True, True)],
                 r=("Rt", "qraw"), w=("psB",), tag=f"rot{head}")
            B.vop(lambda E, hh=hh: E.tensor_copy(qt[hh][:, :], psB[:, 0:1024]),
                  r=("psB",), w=(qk,))
        for hh in range(2):
            head = 2 * g + hh
            kf, kfk = kfull[hh], f"kfull{hh}"
            qk = f"qt{hh}"
            B.act(lambda E, kf=kf: E.activation(scr_b[:, :], kf[:, :],
                                                AF.Square),
                  r=(kfk,), w=("scr_b",))
            for tb in range(4):
                B.mm([(B.ones_col_b[:, :], scr_b[:, tb * 512:(tb + 1) * 512],
                       psD[0:1, 0:512], True, True)],
                     r=("scr_b", "ones_col_b"), w=("psD",))
                B.vop(lambda E, tb=tb: E.tensor_copy(
                    nrow[0:1, tb * 512:(tb + 1) * 512], psD[0:1, 0:512]),
                    r=("psD",), w=("nrow",))
            B.vop(lambda E: E.reduce_max(maxk[0:1, 0:1], nrow[0:1, :],
                                         axis=AX.X), r=("nrow",), w=("maxk",))
            B.act(lambda E: E.activation(maxk[0:1, 0:1], maxk[0:1, 0:1],
                                         AF.Sqrt), r=("maxk",), w=("maxk",))
            B.act(lambda E, hh=hh: E.activation(scr_b[:, 0:1024], qt[hh][:, :],
                                                AF.Square), r=(qk,),
                  w=("scr_b",))
            for tb in range(2):
                B.mm([(B.ones_col_b[:, :], scr_b[:, tb * 512:(tb + 1) * 512],
                       psD[0:1, 0:512], True, True)],
                     r=("scr_b", "ones_col_b"), w=("psD",))
                B.vop(lambda E, tb=tb: E.tensor_copy(
                    crow_f[0:1, tb * 512:(tb + 1) * 512], psD[0:1, 0:512]),
                    r=("psD",), w=("crow_f",))
            B.act(lambda E: E.activation(crow_f[0:1, :], crow_f[0:1, :],
                                         AF.Sqrt), r=("crow_f",), w=("crow_f",))
            B.vop(lambda E: E.tensor_scalar(crow[0:1, :], crow_f[0:1, :],
                                            maxk[0:1, 0:1], 1.0, OP.mult,
                                            OP.add),
                  r=("crow_f", "maxk"), w=("crow",))
            for kc in range(16):
                spk, spt = ("psA", psA) if kc % 2 == 0 else ("psB", psB)
                ptk = f"Pt{kc % 2}"
                pairs = []
                for j in range(2):
                    sl = slice(j * 512, (j + 1) * 512)
                    pairs.append((kf[:, kc * 128:(kc + 1) * 128],
                                  qt[hh][:, sl], spt[:, sl], True, False))
                    pairs.append((B.negones_row_b[:, :], crow[0:1, sl],
                                  spt[:, sl], False, False))
                    pairs.append((B.negones_row_b[:, :],
                                  maskrows[0:1, kc * 1024 + j * 512:
                                           kc * 1024 + (j + 1) * 512],
                                  spt[:, sl], False, True))
                B.mm(pairs, r=(kfk, qk, "crow", "maskrows", "negones_row_b"),
                     w=(spk,), tag=f"s{head}_{kc}")
                dcol = (kc % 8) * 128
                B.vop(lambda E, spt=spt, kc=kc, dcol=dcol: E.tensor_tensor(
                    spt[:, dcol:dcol + 128], tristack[:, kc * 128:(kc + 1) * 128],
                    spt[:, dcol:dcol + 128], OP.add),
                    r=(spk, "tristack"), w=(spk,))
                B.act(lambda E, spt=spt, kc=kc: E.activation(
                    Pt[kc % 2][:, :], spt[:, 0:1024], AF.Exp),
                    r=(spk,), w=(ptk,))
                ypairs = []
                for j in range(2):
                    sl = slice(j * 512, (j + 1) * 512)
                    ypairs.append((vT[hh][:, kc * 128:(kc + 1) * 128],
                                   Pt[kc % 2][:, sl], psY[:, sl],
                                   kc == 0, kc == 15))
                B.mm(ypairs, r=(f"vT{hh}", ptk), w=("psY",), tag=f"ya{head}_{kc}")
                B.mm([(B.ones_col_b[:, :], Pt[kc % 2][:, 0:512],
                       psD[0:1, 0:512], kc == 0, kc == 15)],
                     r=(ptk, "ones_col_b"), w=("psD",), tag=f"dn{head}_{kc}")
                B.mm([(B.ones_col_b[:, :], Pt[kc % 2][:, 512:1024],
                       psE[0:1, 0:512], kc == 0, kc == 15)],
                     r=(ptk, "ones_col_b"), w=("psE",), tag=f"de{head}_{kc}")
            B.vop(lambda E: E.tensor_copy(y_raw[:, :], psY[:, 0:1024]),
                  r=("psY",), w=("y_raw",))
            B.vop(lambda E: E.tensor_copy(den_r[0:1, 0:512], psD[0:1, 0:512]),
                  r=("psD",), w=("den_r",))
            B.vop(lambda E: E.tensor_copy(den_r[0:1, 512:1024], psE[0:1, 0:512]),
                  r=("psE",), w=("den_r",))
            B.vop(lambda E: E.reciprocal(den_r[0:1, :], den_r[0:1, :]),
                  r=("den_r",), w=("den_r",))
            for j in range(2):
                B.mm([(B.ones_row_f[:, :], den_r[0:1, j * 512:(j + 1) * 512],
                       psA[:, j * 512:(j + 1) * 512], True, True)],
                     r=("den_r", "ones_row_f"), w=("psA",))
            B.vop(lambda E, head=head: E.tensor_tensor(
                y_fm[:, head * 1024:(head + 1) * 1024], y_raw[:, :],
                psA[:, 0:1024], OP.mult), r=("y_raw", "psA"), w=("y_fm",))
    S.barrier()
    st4.close()

    st5 = ExitStack()
    psbc = B.ps("psbc5", [128, 512], F32, st5)
    B.psbc_name = "psbc5"
    bo_r = B.load_row(st5, "bo_r", "bo", D)
    x_out = None
    bo_c = B.sb("bo_c", [128, 16], F32, st5)
    for jj in range(16):
        B.row_to_col(bo_r[0:1, jj * 128:(jj + 1) * 128], 128,
                     bo_c[:, jj:jj + 1], "bo_r", "bo_c")
    wo = [B.sb(f"wo{i}", [128, HC * 128], BF16, st5) for i in range(2)]
    xsl = [B.sb(f"xsl{i}", [128, 1024], F32, st5) for i in range(2)]
    psO = B.ps("psO", [128, 1024], F32, st5)
    wo_d = B.din("woT", [D, D])
    for oc in range(16):
        ob = oc % 2
        wok, xk = f"wo{ob}", f"xsl{ob}"
        for hc in range(HC):
            B.dma("g", wo[ob][:, hc * 128:(hc + 1) * 128],
                  wo_d[hc * 128:(hc + 1) * 128, oc * 128:(oc + 1) * 128],
                  r=(), w=(wok,))
        B.dma("y", xsl[ob][:, :],
              bass.AP(B.dram["x_bounce"], oc * 128 * 1024,
                      [[1024, 128], [1, 1024]]),
              r=("x_b",), w=(xk,))
        pairs = []
        for j in range(2):
            sl = slice(j * 512, (j + 1) * 512)
            for hc in range(HC):
                pairs.append((wo[ob][:, hc * 128:(hc + 1) * 128],
                              y_fm[:, hc * 1024 + j * 512:
                                   hc * 1024 + (j + 1) * 512],
                              psO[:, sl], hc == 0, hc == HC - 1))
        B.mm(pairs, r=(wok, "y_fm"), w=("psO",), tag=f"wo{oc}")
        B.vop(lambda E, oc=oc, ob=ob: E.scalar_tensor_tensor(
            xsl[ob][:, :], psO[:, 0:1024], bo_c[:, oc:oc + 1], xsl[ob][:, :],
            OP.add, OP.add), r=("psO", "bo_c", xk), w=(xk,))
        B.dma("y", bass.AP(B.dram["x_bounce"], oc * 128 * 1024,
                           [[1024, 128], [1, 1024]]),
              xsl[ob][:, :], r=(xk,), w=("x_b",), tag=f"xo{oc}")
    S.barrier()
    st5.close()
    st.close()


def phase_l0f(B):
    S = B.S
    st = B.ctx.enter_context(ExitStack())
    B.phase_stack = st
    B.emit_consts(st)

    xq1_fm = B.sb("xq1_fm", [128, HC * 1024], BF16, st)
    invsc1 = B.sb("invsc1", [128, 8], F32, st)
    sc1 = B.sb("sc1", [128, 8], F32, st)
    scg_all = B.sb("scg_all", [128, 8], F32, st)
    scu_all = B.sb("scu_all", [128, 8], F32, st)
    inv2row = B.sb("inv2row", [1, 1024], F32, st)
    swrow = B.sb("swrow", [1, 8], F32, st)
    swcols = B.sb("swcols", [128, 8], F32, st)
    x_b = B.dint("x_bounce", [D, T_CORE])
    inter_d = B.dint("inter_d", [T_CORE, I0])
    xq2_d = B.dint("xq2_d", [T_CORE, I0], BF16)

    wg_d = B.din("wgT", [D, I0])
    wu_d = B.din("wuT", [D, I0])
    wd_d = B.din("wdT", [I0, D])

    st0 = ExitStack()
    wldm = [B.sb(f"wldm{i}", [128, 4096], F32, st0) for i in range(2)]
    macc = B.sb("macc", [128, 3], F32, st0)
    ptmp = B.sb("ptmp", [128, 1], F32, st0)
    ps0 = B.ps("ps0", [128, 512], F32, st0)
    B.psbc_name = "ps0"
    B.vop(lambda E: E.memset(macc[:, :], 0.0), r=(), w=("macc",))
    for mi, wd in enumerate((wg_d, wu_d, wd_d)):
        nr, ncol = wd.shape[0], wd.shape[1]
        cw = min(4096, ncol)
        cnt = 0
        for rc in range(nr // 128):
            for cc in range(ncol // cw):
                bb = cnt % 2
                cnt += 1
                key = f"wldm{bb}"
                B.dma("y", wldm[bb][:, 0:cw],
                      wd[rc * 128:(rc + 1) * 128, cc * cw:(cc + 1) * cw],
                      r=(), w=(key,))
                B.vop(lambda E, bb=bb, cw=cw: E.reduce_sum(
                    ptmp[:, :], wldm[bb][:, 0:cw], axis=AX.X,
                    apply_absolute_value=True), r=(key,), w=("ptmp",))
                B.vop(lambda E, mi=mi: E.tensor_tensor(
                    macc[:, mi:mi + 1], macc[:, mi:mi + 1], ptmp[:, :], OP.add),
                    r=("macc", "ptmp"), w=("macc",))
    for mi in range(3):
        B.mm([(B.ones_col_f[:, :], macc[:, mi:mi + 1], ps0[0:1, 0:1],
               True, True)], r=("macc", "ones_col_f"), w=("ps0",))
        B.vop(lambda E, mi=mi: E.tensor_scalar(
            swrow[0:1, 3 + mi:4 + mi], ps0[0:1, 0:1], 1.0 / (D * I0), 1e-5,
            OP.mult, OP.max), r=("ps0",), w=("swrow",))
    B.vop(lambda E: E.reciprocal(swrow[0:1, 0:3], swrow[0:1, 3:6]),
          r=("swrow",), w=("swrow",))
    B.mm([(B.ones_row_f[:, :], swrow[0:1, 0:8], ps0[:, 0:8], True, True)],
         r=("swrow", "ones_row_f"), w=("ps0",))
    B.vop(lambda E: E.tensor_copy(swcols[:, :], ps0[:, 0:8]),
          r=("ps0",), w=("swcols",))
    S.barrier()
    st0.close()

    st1 = ExitStack()
    xfc = B.sb("xfc", [128, HC * 128], F32, st1)
    x_tm = B.sb("x_tm", [128, D], F32, st1)
    sq1 = B.sb("sq1", [128, D], F32, st1)
    xn1 = B.sb("xn1", [128, D], F32, st1)
    xq1_tm = B.sb("xq1_tm", [128, D], BF16, st1)
    m1 = B.sb("m1", [128, 1], F32, st1)
    q1 = B.sb("q1", [128, 1], F32, st1)
    r1 = B.sb("r1", [128, 1], F32, st1)
    psF = B.ps("psF", [128, 512], F32, st1)
    psX = B.ps("psX", [128, 512], BF16, st1)
    for tc in range(8):
        B.dma("y", xfc[:, :],
              bass.AP(x_b, tc * 128, [[1024, 128], [128 * 1024, HC], [1, 128]]),
              r=("x_b",), w=("xfc",))
        for h0 in range(0, 16, 4):
            for h in range(h0, h0 + 4):
                B.trans(psF[:, (h - h0) * 128:(h - h0 + 1) * 128],
                        xfc[:, h * 128:(h + 1) * 128], B.ident_f[:, :],
                        r=("xfc", "ident_f"), w=("psF",))
            B.vop(lambda E, h0=h0: E.tensor_copy(
                x_tm[:, h0 * 128:(h0 + 4) * 128], psF[:, 0:512]),
                r=("psF",), w=("x_tm",))
        emit_ln_tm(B, x_tm[:, :], D, xn1[:, :], sq1[:, :], "sq1",
                   m1[:, :], q1[:, :], r1[:, :], "l1n", ("x_tm",), ("xn1",))
        B.vop(lambda E: E.reduce_max(m1[:, :], xn1[:, :], axis=AX.X,
                                     apply_absolute_value=True),
              r=("xn1",), w=("l1nm",))
        B.vop(lambda E, tc=tc: E.tensor_scalar(
            invsc1[:, tc:tc + 1], m1[:, :], 1e-5, 1.0 / 128.0, OP.max, OP.mult),
            r=("l1nm",), w=("invsc1",))
        B.vop(lambda E, tc=tc: E.reciprocal(sc1[:, tc:tc + 1],
                                            invsc1[:, tc:tc + 1]),
              r=("invsc1",), w=("sc1",))
        B.vop(lambda E, tc=tc: E.tensor_scalar(
            xn1[:, :], xn1[:, :], sc1[:, tc:tc + 1], MAGIC, OP.mult, OP.add),
            r=("xn1", "sc1"), w=("xn1",))
        B.vop(lambda E: E.tensor_scalar(
            xn1[:, :], xn1[:, :], -MAGIC, -128.0, OP.add, OP.max),
            r=("xn1",), w=("xn1",))
        B.vop(lambda E: E.tensor_single_scalar(xq1_tm[:, :], xn1[:, :], 127.0,
                                               OP.min), r=("xn1",), w=("xq1_tm",))
        for h0 in range(0, 16, 4):
            for h in range(h0, h0 + 4):
                B.trans(psX[:, (h - h0) * 128:(h - h0 + 1) * 128],
                        xq1_tm[:, h * 128:(h + 1) * 128], B.ident_b[:, :],
                        r=("xq1_tm", "ident_b"), w=("psX",))
            B.vop(lambda E, h0=h0, tc=tc: E.tensor_copy(
                apn(xq1_fm, 0, 128, h0 * 1024 + tc * 128, [(1024, 4), (1, 128)]),
                psX[:, 0:512]), r=("psX",), w=("xq1_fm",))
    B.vop(lambda E: E.tensor_scalar_mul(scg_all[:, :], invsc1[:, :],
                                        swcols[:, 3:4]),
          r=("invsc1", "swcols"), w=("scg_all",))
    B.vop(lambda E: E.tensor_scalar_mul(scu_all[:, :], invsc1[:, :],
                                        swcols[:, 4:5]),
          r=("invsc1", "swcols"), w=("scu_all",))
    S.barrier()
    st1.close()

    stA = ExitStack()
    tern_g = B.sb("tern_g", [128, HC * 512], BF16, stA)
    tern_u = B.sb("tern_u", [128, HC * 512], BF16, stA)
    wldA = [B.sb(f"wldA{i}", [128, 512], F32, stA) for i in range(2)]
    gt = B.sb("gt", [128, 512], F32, stA)
    ut = B.sb("ut", [128, 512], F32, stA)
    sg_t = B.sb("sg_t", [128, 512], F32, stA)
    it_t = [B.sb(f"it_t{i}", [128, 512], F32, stA) for i in range(2)]
    psG = [B.ps(f"psG{i}", [128, 512], F32, stA) for i in range(2)]
    psU = [B.ps(f"psU{i}", [128, 512], F32, stA) for i in range(2)]

    def quant_w(dst, dkey, wsrc, swcol_idx, bb, tag):
        key = f"wldA{bb}"
        B.dma("y", wldA[bb][:, :], wsrc, r=(), w=(key,), tag=tag)
        B.vop(lambda E: E.tensor_scalar(
            wldA[bb][:, :], wldA[bb][:, :], swcols[:, swcol_idx:swcol_idx + 1],
            MAGIC, OP.mult, OP.add), r=(key, "swcols"), w=(key,))
        B.vop(lambda E: E.tensor_scalar(
            wldA[bb][:, :], wldA[bb][:, :], -MAGIC, -1.0, OP.add, OP.max),
            r=(key,), w=(key,))
        B.vop(lambda E: E.tensor_single_scalar(dst, wldA[bb][:, :], 1.0,
                                               OP.min), r=(key,), w=(dkey,))

    for ib in range(16):
        for hc in range(HC):
            quant_w(tern_g[:, hc * 512:(hc + 1) * 512], "tern_g",
                    wg_d[hc * 128:(hc + 1) * 128, ib * 512:(ib + 1) * 512],
                    0, hc % 2, f"qg{ib}_{hc}")
            quant_w(tern_u[:, hc * 512:(hc + 1) * 512], "tern_u",
                    wu_d[hc * 128:(hc + 1) * 128, ib * 512:(ib + 1) * 512],
                    1, hc % 2, f"qu{ib}_{hc}")
        for tc in range(8):
            pb = tc % 2
            gpairs, upairs = [], []
            for hc in range(HC):
                a = xq1_fm[:, hc * 1024 + tc * 128: hc * 1024 + (tc + 1) * 128]
                gpairs.append((a, tern_g[:, hc * 512:(hc + 1) * 512],
                               psG[pb][:, 0:512], hc == 0, hc == HC - 1))
                upairs.append((a, tern_u[:, hc * 512:(hc + 1) * 512],
                               psU[pb][:, 0:512], hc == 0, hc == HC - 1))
            B.mm(gpairs, r=("xq1_fm", "tern_g"), w=(f"psG{pb}",), tag=f"g{ib}_{tc}")
            B.mm(upairs, r=("xq1_fm", "tern_u"), w=(f"psU{pb}",), tag=f"u{ib}_{tc}")
            B.vop(lambda E, pb=pb, tc=tc: E.tensor_scalar_mul(
                gt[:, :], psG[pb][:, 0:512], scg_all[:, tc:tc + 1]),
                r=(f"psG{pb}", "scg_all"), w=("gt",))
            B.act(lambda E: E.activation(sg_t[:, :], gt[:, :], AF.Silu),
                  r=("gt",), w=("sg_t",))
            B.vop(lambda E, pb=pb, tc=tc: E.tensor_scalar_mul(
                ut[:, :], psU[pb][:, 0:512], scu_all[:, tc:tc + 1]),
                r=(f"psU{pb}", "scu_all"), w=("ut",))
            B.vop(lambda E, pb=pb: E.tensor_mul(it_t[pb][:, :], sg_t[:, :],
                                                ut[:, :]),
                  r=("sg_t", "ut"), w=(f"it_t{pb}",))
            B.dma("y", inter_d[tc * 128:(tc + 1) * 128, ib * 512:(ib + 1) * 512],
                  it_t[pb][:, :], r=(f"it_t{pb}",), w=("inter_d",),
                  tag=f"si{ib}_{tc}")
    S.barrier()
    stA.close()

    stB = ExitStack()
    inA = B.sb("inA", [128, I0], F32, stB)
    inB = B.sb("inB", [128, I0], F32, stB)
    xq2_tm = B.sb("xq2_tm", [128, I0], BF16, stB)
    mB = B.sb("mB", [128, 1], F32, stB)
    iB = B.sb("iB", [128, 1], F32, stB)
    psV = B.ps("psV", [128, 512], F32, stB)
    B.psbc_name = "psV"
    for tc in range(8):
        B.dma("y", inA[:, :], inter_d[tc * 128:(tc + 1) * 128, :],
              r=("inter_d",), w=("inA",))
        cur, nxt = ("inA", "inB")
        for sidx in range(13):
            h = 1 << sidx
            ct, nt = B.t[cur], B.t[nxt]
            pat = [(2 * h, 4096 // h), (1, h)]
            B.vop(lambda E, ct=ct, nt=nt, h=h, pat=pat: E.tensor_tensor(
                apn(nt, 0, 128, 0, pat), apn(ct, 0, 128, 0, pat),
                apn(ct, 0, 128, h, pat), OP.add), r=(cur,), w=(nxt,))
            B.vop(lambda E, ct=ct, nt=nt, h=h, pat=pat: E.tensor_tensor(
                apn(nt, 0, 128, h, pat), apn(ct, 0, 128, 0, pat),
                apn(ct, 0, 128, h, pat), OP.subtract), r=(cur,), w=(nxt,))
            cur, nxt = nxt, cur
        ft = B.t[cur]
        B.vop(lambda E, ft=ft: E.reduce_max(mB[:, :], ft[:, :], axis=AX.X,
                                            apply_absolute_value=True),
              r=(cur,), w=("mB",))
        B.vop(lambda E: E.tensor_scalar(iB[:, :], mB[:, :],
                                        float(1e-5 * np.sqrt(float(I0))),
                                        1.0 / 128.0,
                                        OP.max, OP.mult), r=("mB",), w=("iB",))
        B.vop(lambda E, tc=tc: E.tensor_copy(mB[:, :], iB[:, :]),
              r=("iB",), w=("mB",))
        B.vop(lambda E: E.reciprocal(mB[:, :], mB[:, :]), r=("mB",), w=("mB",))
        B.vop(lambda E, ft=ft: E.tensor_scalar(
            ft[:, :], ft[:, :], mB[:, :], MAGIC, OP.mult, OP.add),
            r=(cur, "mB"), w=(cur,))
        B.vop(lambda E, ft=ft: E.tensor_scalar(
            ft[:, :], ft[:, :], -MAGIC, -128.0, OP.add, OP.max),
            r=(cur,), w=(cur,))
        B.vop(lambda E, ft=ft: E.tensor_single_scalar(
            xq2_tm[:, :], ft[:, :], 127.0, OP.min), r=(cur,), w=("xq2_tm",))
        B.dma("y", xq2_d[tc * 128:(tc + 1) * 128, :], xq2_tm[:, :],
              r=("xq2_tm",), w=("xq2_d",))
        B.trans(psV[0:1, 0:128], iB[:, 0:1], B.ident_f[:, :],
                r=("iB", "ident_f"), w=("psV",))
        B.vop(lambda E, tc=tc: E.tensor_copy(
            inv2row[0:1, tc * 128:(tc + 1) * 128], psV[0:1, 0:128]),
            r=("psV",), w=("inv2row",))
    S.barrier()
    stB.close()

    stC = ExitStack()
    xq2h = B.sb("xq2h", [128, 64 * 512], BF16, stC)
    stg = B.sb("stg", [128, I0], BF16, stC)
    tern_d = [B.sb(f"tern_d{i}", [128, 64 * 128], BF16, stC) for i in range(2)]
    wldC = [B.sb(f"wldC{i}", [128, 512], F32, stC) for i in range(2)]
    sc2b = B.sb("sc2b", [128, 512], F32, stC)
    s2row = B.sb("s2row", [1, 512], F32, stC)
    xsl = [B.sb(f"xslC{i}", [128, 512], F32, stC) for i in range(2)]
    ot = [B.sb(f"ot{i}", [128, 512], F32, stC) for i in range(2)]
    psC = [B.ps(f"psCC{i}", [128, 512], F32, stC) for i in range(2)]
    psT = B.ps("psTT", [128, 512], BF16, stC)
    B.psbc_name = "psCC0"
    RSQ = float(1.0 / np.sqrt(float(I0)))
    for tb in range(2):
        for tci in range(4):
            tc = tb * 4 + tci
            B.dma("y", stg[:, :], xq2_d[tc * 128:(tc + 1) * 128, :],
                  r=("xq2_d",), w=("stg",))
            for i0 in range(0, 64, 4):
                for ic in range(i0, i0 + 4):
                    B.trans(psT[:, (ic - i0) * 128:(ic - i0 + 1) * 128],
                            stg[:, ic * 128:(ic + 1) * 128], B.ident_b[:, :],
                            r=("stg", "ident_b"), w=("psT",))
                B.vop(lambda E, i0=i0, tci=tci: E.tensor_copy(
                    apn(xq2h, 0, 128, i0 * 512 + tci * 128,
                        [(512, 4), (1, 128)]), psT[:, 0:512]),
                    r=("psT",), w=("xq2h",))
        B.vop(lambda E, tb=tb: E.tensor_scalar(
            s2row[0:1, :], inv2row[0:1, tb * 512:(tb + 1) * 512],
            swrow[0:1, 5:6], RSQ, OP.mult, OP.mult),
            r=("inv2row", "swrow"), w=("s2row",))
        B.mm([(B.ones_row_f[:, :], s2row[0:1, 0:512], psC[0][:, 0:512],
               True, True)], r=("s2row", "ones_row_f"), w=("psCC0",))
        B.vop(lambda E: E.tensor_copy(sc2b[:, :], psC[0][:, 0:512]),
              r=("psCC0",), w=("sc2b",))
        for oc in range(16):
            ob = oc % 2
            tdk = f"tern_d{ob}"
            for p4 in range(16):
                bb = p4 % 2
                key = f"wldC{bb}"
                B.dma("y", wldC[bb][:, :],
                      bass.AP(wd_d, (p4 * 4 * 128) * D + oc * 128,
                              [[D, 128], [128 * D, 4], [1, 128]]),
                      r=(), w=(key,), tag=f"wd{oc}_{p4}")
                B.vop(lambda E, bb=bb: E.tensor_scalar(
                    wldC[bb][:, :], wldC[bb][:, :], swcols[:, 2:3], MAGIC,
                    OP.mult, OP.add), r=(key, "swcols"), w=(key,))
                B.vop(lambda E, bb=bb: E.tensor_scalar(
                    wldC[bb][:, :], wldC[bb][:, :], -MAGIC, -1.0,
                    OP.add, OP.max), r=(key,), w=(key,))
                B.vop(lambda E, bb=bb, p4=p4, ob=ob: E.tensor_single_scalar(
                    tern_d[ob][:, p4 * 512:(p4 + 1) * 512], wldC[bb][:, :],
                    1.0, OP.min), r=(key,), w=(tdk,))
            pairs = []
            for ic in range(64):
                pairs.append((tern_d[ob][:, ic * 128:(ic + 1) * 128],
                              xq2h[:, ic * 512:(ic + 1) * 512],
                              psC[ob][:, 0:512], ic == 0, ic == 63))
            B.mm(pairs, r=(tdk, "xq2h"), w=(f"psCC{ob}",), tag=f"dn{tb}_{oc}")
            B.dma("y", xsl[ob][:, :],
                  bass.AP(x_b, oc * 128 * 1024 + tb * 512,
                          [[1024, 128], [1, 512]]),
                  r=("x_b",), w=(f"xslC{ob}",))
            B.vop(lambda E, ob=ob: E.tensor_mul(ot[ob][:, :],
                                                psC[ob][:, 0:512], sc2b[:, :]),
                  r=(f"psCC{ob}", "sc2b"), w=(f"ot{ob}",))
            B.vop(lambda E, ob=ob: E.tensor_tensor(
                ot[ob][:, :], ot[ob][:, :], xsl[ob][:, :], OP.add),
                r=(f"ot{ob}", f"xslC{ob}"), w=(f"ot{ob}",))
            B.dma("y", bass.AP(x_b, oc * 128 * 1024 + tb * 512,
                               [[1024, 128], [1, 512]]), ot[ob][:, :],
                  r=(f"ot{ob}",), w=("x_b",), tag=f"xw{tb}_{oc}")
    S.barrier()
    stC.close()
    st.close()


def emit_l1a_main(B, st, k_phi, kv_bf, x_b, q_d, carry_i, wout_d, xslice):
    S = B.S
    stS = ExitStack()
    drow = B.sb("drow", [1, NH], F32, stS)
    lnrow = B.sb("lnrow", [1, NH], F32, stS)
    lnd_col = B.sb("lnd_col", [128, 1], F32, stS)
    tri01 = B.sb("tri01", [128, 128], F32, stS)
    iotaE = B.sb("iotaE", [128, 128], F32, stS)
    sc_ud = B.sb("sc_ud", [128, 8], F32, stS)
    sc_dv = B.sb("sc_dv", [1, 128], F32, stS)
    sc_dvg = B.sb("sc_dvg", [1, 1024], F32, stS)
    Ub = B.sb("Ub", [128, 128], BF16, stS)
    udb = B.sb("udb", [128, 8 * NH], BF16, stS)
    dvb = B.sb("dvb", [1, 128], BF16, stS)
    dvgb = B.sb("dvgb", [1, 1024], BF16, stS)
    uscr = B.sb("uscr", [128, 128], F32, stS)
    carrys = B.sb("carrys", [1, 2 * 1024], BF16, stS)
    carry_al = B.sb("carry_al", [1, 4096], F32, stS)
    carry_eff = B.sb("carry_eff", [1, 4096], BF16, stS)
    rkv = B.sb("rkv", [128, 8 * 128], F32, stS)
    rks = B.sb("rks", [128, 8 * 128], F32, stS)
    qh = B.sb("qh", [128, 8 * 128], BF16, stS)
    dall = B.sb("dall", [128, 8], F32, stS)
    tt1 = B.sb("tt1", [128, 8 * 128], F32, stS)
    out_tm = B.sb("out_tm", [128, 8 * 128], BF16, stS)
    y_fm = B.sb("y_fm", [128, HC * 1024], BF16, stS)
    psCar = [B.ps(f"psCar{i}", [128, 512], F32, stS) for i in range(2)]
    psS = [B.ps(f"psS{i}", [128, 512], F32, stS) for i in range(2)]
    psT = B.ps("psTb", [128, 512], BF16, stS)

    B.dma("y", drow[:, :], B.din("decay", [1, NH])[:, :], r=(), w=("drow",))
    B.dma("y", iotaE[:, :], B.din("iotaE", [128, 128])[:, :], r=(), w=("iotaE",))
    B.dma("y", sc_ud[:, :], B.din("sc_udelta", [128, 8])[:, :], r=(), w=("sc_ud",))
    B.dma("y", sc_dv[:, :], B.din("sc_dvec", [1, 128])[:, :], r=(), w=("sc_dv",))
    B.dma("y", sc_dvg[:, :], B.din("sc_dvecg", [1, 1024])[:, :], r=(), w=("sc_dvg",))
    B.dma("y", carry_al[0:1, :], carry_i[0:1, :], r=(), w=("carry_al",))
    B.vop(lambda E: E.tensor_copy(carry_eff[0:1, :], carry_al[0:1, :]),
          r=("carry_al",), w=("carry_eff",))
    B.act(lambda E: E.activation(drow[0:1, :], drow[0:1, :], AF.Sigmoid),
          r=("drow",), w=("drow",))
    B.act(lambda E: E.activation(lnrow[0:1, :], drow[0:1, :], AF.Ln),
          r=("drow",), w=("lnrow",))
    B.vop(lambda E: E.tensor_single_scalar(tri01[:, :], iotaE[:, :], 0.0,
                                           OP.is_ge), r=("iotaE",), w=("tri01",))
    for h in range(NH):
        B.mm([(B.ones_row_f[:, :], lnrow[0:1, h:h + 1], psCar[0][:, 0:1],
               True, True)], r=("lnrow", "ones_row_f"), w=("psCar0",))
        B.vop(lambda E: E.tensor_copy(lnd_col[:, :], psCar[0][:, 0:1]),
              r=("psCar0",), w=("lnd_col",))
        B.vop(lambda E: E.tensor_scalar_mul(uscr[:, 0:8], sc_ud[:, :],
                                            lnd_col[:, 0:1]),
              r=("sc_ud", "lnd_col"), w=("uscr",))
        B.act(lambda E, h=h: E.activation(udb[:, h * 8:(h + 1) * 8],
                                          uscr[:, 0:8], AF.Exp),
              r=("uscr",), w=("udb",))
    S.barrier()

    wout_sb = [B.sb(f"wout{i}", [128, HC * 128], BF16, stS) for i in range(2)]
    xslW = [B.sb(f"xslW{i}", [128, 1024], F32, stS) for i in range(2)]
    for h in range(NH):
        B.mm([(B.ones_row_f[:, :], lnrow[0:1, h:h + 1], psCar[0][:, 0:1],
               True, True)], r=("lnrow", "ones_row_f"), w=("psCar0",))
        B.vop(lambda E: E.tensor_copy(lnd_col[:, :], psCar[0][:, 0:1]),
              r=("psCar0",), w=("lnd_col",))
        B.vop(lambda E: E.tensor_scalar_mul(uscr[:, :], iotaE[:, :],
                                            lnd_col[:, 0:1]),
              r=("iotaE", "lnd_col"), w=("uscr",))
        B.act(lambda E: E.activation(uscr[:, :], uscr[:, :], AF.Exp),
              r=("uscr",), w=("uscr",))
        B.vop(lambda E: E.tensor_tensor(Ub[:, :], uscr[:, :], tri01[:, :],
                                        OP.mult),
              r=("uscr", "tri01"), w=("Ub",))
        B.vop(lambda E, h=h: E.tensor_scalar_mul(
            uscr[0:1, 0:128], sc_dv[0:1, :], lnrow[0:1, h:h + 1]),
            r=("sc_dv", "lnrow"), w=("uscr",))
        B.act(lambda E: E.activation(dvb[0:1, :], uscr[0:1, 0:128], AF.Exp),
              r=("uscr",), w=("dvb",))
        B.vop(lambda E, h=h: E.tensor_scalar_mul(
            tt1[0:1, 0:1024], sc_dvg[0:1, :], lnrow[0:1, h:h + 1]),
            r=("sc_dvg", "lnrow"), w=("tt1",))
        B.act(lambda E: E.activation(dvgb[0:1, :], tt1[0:1, 0:1024], AF.Exp),
              r=("tt1",), w=("dvgb",))
        for ki, (srck, src, dstk, dst) in enumerate(
                (("kv_bf", kv_bf, "rkv", rkv), ("k_phi", k_phi, "rks", rks))):
            for c in range(1, 8):
                pairs = []
                for cp in range(c):
                    pairs.append((udb[:, h * 8 + (c - cp - 1):h * 8 + (c - cp)],
                                  src[:, cp * D + h * 128: cp * D + (h + 1) * 128],
                                  psCar[c % 2][0:1, (c % 4) * 128:
                                               (c % 4 + 1) * 128],
                                  cp == 0, cp == c - 1))
                B.mm(pairs, r=(srck, "udb"), w=(f"psCar{c % 2}",),
                     tag=f"lc{ki}_{h}_{c}")
                B.vop(lambda E, ki=ki, c=c: E.tensor_copy(
                    carrys[0:1, ki * 1024 + c * 128: ki * 1024 + (c + 1) * 128],
                    psCar[c % 2][0:1, (c % 4) * 128:(c % 4 + 1) * 128]),
                    r=(f"psCar{c % 2}",), w=("carrys",))
            for cg in range(2):
                pk = f"psS{cg}"
                pairs = []
                for ci in range(4):
                    c = cg * 4 + ci
                    reg = psS[cg][:, ci * 128:(ci + 1) * 128]
                    pairs.append((Ub[:, :],
                                  src[:, c * D + h * 128: c * D + (h + 1) * 128],
                                  reg, True, False))
                    if c > 0:
                        pairs.append((dvb[0:1, :],
                                      carrys[0:1, ki * 1024 + c * 128:
                                             ki * 1024 + (c + 1) * 128],
                                      reg, False, False))
                    pairs.append((dvgb[0:1, c * 128:(c + 1) * 128],
                                  carry_eff[0:1, ki * 2048 + h * 128:
                                            ki * 2048 + (h + 1) * 128],
                                  reg, False, True))
                B.mm(pairs, r=(srck, "Ub", "dvb", "dvgb", "carrys",
                               "carry_eff"), w=(pk,), tag=f"sc{ki}_{h}_{cg}")
                B.vop(lambda E, pk=pk, cg=cg, dst=dst, dstk=dstk:
                      E.tensor_copy(dst[:, cg * 512:(cg + 1) * 512],
                                    B.t[pk][:, 0:512]), r=(pk,), w=(dstk,))
        B.dma("y", qh[:, :],
              bass.AP(q_d, h * 128, [[D, 128], [128 * D, 8], [1, 128]]),
              r=("q_d",), w=("qh",), tag=f"qr{h}")
        B.vop(lambda E: E.tensor_mul(tt1[:, :], qh[:, :], rks[:, :]),
              r=("qh", "rks"), w=("tt1",))
        for tc in range(8):
            B.vop(lambda E, tc=tc: E.reduce_sum(
                dall[:, tc:tc + 1], tt1[:, tc * 128:(tc + 1) * 128], axis=AX.X),
                r=("tt1",), w=("dall",))
        B.vop(lambda E: E.tensor_single_scalar(dall[:, :], dall[:, :], 1e-6,
                                               OP.max), r=("dall",), w=("dall",))
        B.vop(lambda E: E.reciprocal(dall[:, :], dall[:, :]),
              r=("dall",), w=("dall",))
        B.vop(lambda E: E.tensor_mul(tt1[:, :], qh[:, :], rkv[:, :]),
              r=("qh", "rkv"), w=("tt1",))
        for tc in range(8):
            B.vop(lambda E, tc=tc: E.tensor_scalar_mul(
                out_tm[:, tc * 128:(tc + 1) * 128],
                tt1[:, tc * 128:(tc + 1) * 128], dall[:, tc:tc + 1]),
                r=("tt1", "dall"), w=("out_tm",))
        for tc4 in range(2):
            for tci in range(4):
                tc = tc4 * 4 + tci
                B.trans(psT[:, tci * 128:(tci + 1) * 128],
                        out_tm[:, tc * 128:(tc + 1) * 128], B.ident_b[:, :],
                        r=("out_tm", "ident_b"), w=("psTb",))
            B.vop(lambda E, h=h, tc4=tc4: E.tensor_copy(
                apn(y_fm, 0, 128, h * 1024 + tc4 * 512, [(128, 4), (1, 128)]),
                psT[:, 0:512]), r=("psTb",), w=("y_fm",))
    S.barrier()

    psO = B.ps("psO1", [128, 1024], F32, stS)
    for oc in range(16):
        ob = oc % 2
        wk = f"wout{ob}"
        for hc in range(HC):
            B.dma("g", wout_sb[ob][:, hc * 128:(hc + 1) * 128],
                  wout_d[hc * 128:(hc + 1) * 128, oc * 128:(oc + 1) * 128],
                  r=(), w=(wk,))
        pairs = []
        for j in range(2):
            sl = slice(j * 512, (j + 1) * 512)
            for hc in range(HC):
                pairs.append((wout_sb[ob][:, hc * 128:(hc + 1) * 128],
                              y_fm[:, hc * 1024 + j * 512:
                                   hc * 1024 + (j + 1) * 512],
                              psO[:, sl], hc == 0, hc == HC - 1))
        B.mm(pairs, r=(wk, "y_fm"), w=("psO1",), tag=f"wo1_{oc}")
        xk = f"xslW{ob}"
        B.dma("y", xslW[ob][:, :], xslice(oc), r=("x_b",), w=(xk,))
        B.vop(lambda E, oc=oc, ob=ob: E.tensor_tensor(
            xslW[ob][:, :], xslW[ob][:, :], psO[:, 0:1024], OP.add),
            r=("psO1", xk), w=(xk,))
        B.dma("y", xslice(oc), xslW[ob][:, :], r=(xk,), w=("x_b",),
              tag=f"xw1_{oc}")
    S.barrier()
    stS.close()


def phase_l1a(B, mode="host"):
    S = B.S
    st = B.ctx.enter_context(ExitStack())
    B.phase_stack = st
    B.emit_consts(st)

    k_phi = B.sb("k_phi", [128, 8 * D], BF16, st)
    kv_bf = B.sb("kv_bf", [128, 8 * D], BF16, st)
    x_b = B.dint("x_bounce", [D, T_CORE])
    if mode == "pre":
        q_d = B.dout("q_io", [T_CORE, D], BF16)
        kphi_d = B.dout("kphi_io", [128, 8 * D], BF16)
        kv_d = B.dout("kv_io", [128, 8 * D], BF16)
        carry_o = B.dout("carry_out", [1, 4096])
    else:
        q_d = B.din("q_io", [T_CORE, D], BF16)
        kphi_d = B.din("kphi_io", [128, 8 * D], BF16)
        kv_d = B.din("kv_io", [128, 8 * D], BF16)
        carry_i = B.din("carry_in", [1, 4096])
    wqkv_d = B.din("wqkv", [D, 3 * D])
    wout_d = B.din("wout", [D, D])

    def xslice(hc):
        return bass.AP(x_b, hc * 128 * 1024, [[1024, 128], [1, 1024]])

    if mode == "main":
        B.dma("y", k_phi[:, :], kphi_d[:, :], r=(), w=("k_phi",))
        B.dma("y", kv_bf[:, :], kv_d[:, :], r=(), w=("kv_bf",))
        emit_l1a_main(B, st, k_phi, kv_bf, x_b, q_d, carry_i, wout_d, xslice)
        st.close()
        return

    stN = ExitStack()
    xn2 = B.sb("xn2", [128, HC * 1024], BF16, stN)
    stN2 = ExitStack()
    xsl = [B.sb(f"xslN{i}", [128, 1024], F32, stN2) for i in range(2)]
    xsq = B.sb("xsqN", [128, 1024], F32, stN2)
    tmpN = B.sb("tmpN", [128, 1024], F32, stN2)
    mrow = B.sb("mrow", [1, 1024], F32, stN2)
    rrow = B.sb("rrow", [1, 1024], F32, stN2)
    psSu = [B.ps(f"psSu{i}", [128, 512], F32, stN2) for i in range(2)]
    psSq = [B.ps(f"psSq{i}", [128, 512], F32, stN2) for i in range(2)]
    psMu = B.ps("psMu", [128, 1024], F32, stN2)
    psRr = B.ps("psRr", [128, 1024], F32, stN2)
    for hc in range(HC):
        hb = hc % 2
        xk = f"xslN{hb}"
        B.dma("y", xsl[hb][:, :], xslice(hc), r=("x_b",), w=(xk,))
        B.act(lambda E, hb=hb: E.activation(
            xsq[:, :], xsl[hb][:, :], AF.Square), r=(xk,), w=("xsqN",))
        for j in range(2):
            B.mm([(B.ones_col_f[:, :],
                   xsl[hb][:, j * 512:(j + 1) * 512],
                   psSu[j][0:1, 0:512], hc == 0, hc == HC - 1)],
                 r=(xk, "ones_col_f"), w=(f"psSu{j}",))
            B.mm([(B.ones_col_f[:, :], xsq[:, j * 512:(j + 1) * 512],
                   psSq[j][0:1, 0:512], hc == 0, hc == HC - 1)],
                 r=("xsqN", "ones_col_f"), w=(f"psSq{j}",))
    for j in range(2):
        B.vop(lambda E, j=j: E.tensor_single_scalar(
            mrow[0:1, j * 512:(j + 1) * 512], psSu[j][0:1, 0:512],
            1.0 / D, OP.mult), r=(f"psSu{j}",), w=("mrow",))
        B.vop(lambda E, j=j: E.tensor_single_scalar(
            rrow[0:1, j * 512:(j + 1) * 512], psSq[j][0:1, 0:512],
            1.0 / D, OP.mult), r=(f"psSq{j}",), w=("rrow",))
    B.vop(lambda E: E.tensor_mul(tmpN[0:1, :], mrow[0:1, :], mrow[0:1, :]),
          r=("mrow",), w=("tmpN",))
    B.vop(lambda E: E.scalar_tensor_tensor(
        rrow[0:1, :], tmpN[0:1, :], -1.0, rrow[0:1, :], OP.mult, OP.add),
        r=("tmpN", "rrow"), w=("rrow",))
    B.vop(lambda E: E.tensor_single_scalar(rrow[0:1, :], rrow[0:1, :], EPS,
                                           OP.add), r=("rrow",), w=("rrow",))
    B.act(lambda E: E.activation(rrow[0:1, :], rrow[0:1, :], AF.Sqrt),
          r=("rrow",), w=("rrow",))
    B.vop(lambda E: E.reciprocal(rrow[0:1, :], rrow[0:1, :]),
          r=("rrow",), w=("rrow",))
    for j in range(2):
        B.mm([(B.ones_row_f[:, :], mrow[0:1, j * 512:(j + 1) * 512],
               psMu[:, j * 512:(j + 1) * 512], True, True)],
             r=("mrow", "ones_row_f"), w=("psMu",))
        B.mm([(B.ones_row_f[:, :], rrow[0:1, j * 512:(j + 1) * 512],
               psRr[:, j * 512:(j + 1) * 512], True, True)],
             r=("rrow", "ones_row_f"), w=("psRr",))
    for hc in range(HC):
        hb = hc % 2
        xk = f"xslN{hb}"
        B.dma("y", xsl[hb][:, :], xslice(hc), r=("x_b",), w=(xk,))
        B.vop(lambda E, hb=hb: E.tensor_sub(
            tmpN[:, :], xsl[hb][:, :], psMu[:, 0:1024]),
            r=(xk, "psMu"), w=("tmpN",))
        B.vop(lambda E, hc=hc: E.tensor_tensor(
            xn2[:, hc * 1024:(hc + 1) * 1024], tmpN[:, :], psRr[:, 0:1024],
            OP.mult), r=("tmpN", "psRr"), w=("xn2",))
    S.barrier()
    stN2.close()

    stQ = ExitStack()
    wq = [B.sb(f"wqv{i}", [128, HC * 512], BF16, stQ) for i in range(2)]
    et = B.sb("et", [128, 512], F32, stQ)
    qb_t = B.sb("qb_t", [128, 512], BF16, stQ)
    psQ = [B.ps(f"psQk{i}", [128, 512], F32, stQ) for i in range(2)]
    for ng in range(12):
        nb = ng % 2
        wk = f"wqv{nb}"
        for hc in range(HC):
            B.dma("g", wq[nb][:, hc * 512:(hc + 1) * 512],
                  wqkv_d[hc * 128:(hc + 1) * 128, ng * 512:(ng + 1) * 512],
                  r=(), w=(wk,))
        for tc in range(8):
            pb = tc % 2
            pk = f"psQk{pb}"
            pairs = []
            for hc in range(HC):
                pairs.append((xn2[:, hc * 1024 + tc * 128:
                                  hc * 1024 + (tc + 1) * 128],
                              wq[nb][:, hc * 512:(hc + 1) * 512],
                              psQ[pb][:, 0:512], hc == 0, hc == HC - 1))
            B.mm(pairs, r=("xn2", wk), w=(pk,), tag=f"qkv{ng}_{tc}")
            if ng < 8:
                B.act(lambda E, pb=pb: E.activation(et[:, :], psQ[pb][:, 0:512],
                                                    AF.Exp),
                      r=(pk,), w=("et",))
                B.vop(lambda E: E.tensor_single_scalar(et[:, :], et[:, :], 1.0,
                                                       OP.min),
                      r=("et",), w=("et",))
                if ng < 4:
                    B.vop(lambda E, pb=pb: E.scalar_tensor_tensor(
                        qb_t[:, :], psQ[pb][:, 0:512], 0.0, et[:, :],
                        OP.max, OP.add), r=(pk, "et"), w=("qb_t",))
                    B.dma("y", q_d[tc * 128:(tc + 1) * 128,
                                   ng * 512:(ng + 1) * 512], qb_t[:, :],
                          r=("qb_t",), w=("q_d",), tag=f"qs{ng}_{tc}")
                else:
                    s0 = (ng - 4) * 512
                    B.vop(lambda E, pb=pb, tc=tc, s0=s0: E.scalar_tensor_tensor(
                        k_phi[:, tc * D + s0: tc * D + s0 + 512],
                        psQ[pb][:, 0:512], 0.0, et[:, :], OP.max, OP.add),
                        r=(pk, "et"), w=("k_phi",))
            else:
                s0 = (ng - 8) * 512
                B.vop(lambda E, pb=pb, tc=tc, s0=s0: E.tensor_tensor(
                    kv_bf[:, tc * D + s0: tc * D + s0 + 512],
                    k_phi[:, tc * D + s0: tc * D + s0 + 512],
                    psQ[pb][:, 0:512], OP.mult),
                    r=(pk, "k_phi"), w=("kv_bf",))
    S.barrier()
    stQ.close()
    stN.close()

    stS = ExitStack()
    drow = B.sb("drow", [1, NH], F32, stS)
    lnrow = B.sb("lnrow", [1, NH], F32, stS)
    lnd_col = B.sb("lnd_col", [128, 1], F32, stS)
    tri01 = B.sb("tri01", [128, 128], F32, stS)
    iotaE = B.sb("iotaE", [128, 128], F32, stS)
    sc_ud = B.sb("sc_ud", [128, 8], F32, stS)
    sc_dv = B.sb("sc_dv", [1, 128], F32, stS)
    sc_dvg = B.sb("sc_dvg", [1, 1024], F32, stS)
    Ub = B.sb("Ub", [128, 128], BF16, stS)
    udb = B.sb("udb", [128, 8 * NH], BF16, stS)
    dvb = B.sb("dvb", [1, 128], BF16, stS)
    dvgb = B.sb("dvgb", [1, 1024], BF16, stS)
    uscr = B.sb("uscr", [128, 128], F32, stS)
    carrys = B.sb("carrys", [1, 2 * 1024], BF16, stS)
    carry_all = B.sb("carry_all", [1, 4096], F32, stS)
    carry_eff = B.sb("carry_eff", [1, 4096], BF16, stS)
    halfm = B.load_row(stS, "halfm", "halfmask", 1)
    rkv = B.sb("rkv", [128, 8 * 128], F32, stS)
    rks = B.sb("rks", [128, 8 * 128], F32, stS)
    qh = B.sb("qh", [128, 8 * 128], BF16, stS)
    dall = B.sb("dall", [128, 8], F32, stS)
    tt1 = B.sb("tt1", [128, 8 * 128], F32, stS)
    out_tm = B.sb("out_tm", [128, 8 * 128], BF16, stS)
    y_fm = B.sb("y_fm", [128, HC * 1024], BF16, stS)
    psCar = [B.ps(f"psCar{i}", [128, 512], F32, stS) for i in range(2)]
    psS = [B.ps(f"psS{i}", [128, 512], F32, stS) for i in range(2)]
    psT = B.ps("psTb", [128, 512], BF16, stS)

    B.dma("y", drow[:, :], B.din("decay", [1, NH])[:, :], r=(), w=("drow",))
    B.dma("y", iotaE[:, :], B.din("iotaE", [128, 128])[:, :], r=(), w=("iotaE",))
    B.dma("y", sc_ud[:, :], B.din("sc_udelta", [128, 8])[:, :], r=(), w=("sc_ud",))
    B.dma("y", sc_dv[:, :], B.din("sc_dvec", [1, 128])[:, :], r=(), w=("sc_dv",))
    B.dma("y", sc_dvg[:, :], B.din("sc_dvecg", [1, 1024])[:, :], r=(), w=("sc_dvg",))
    B.act(lambda E: E.activation(drow[0:1, :], drow[0:1, :], AF.Sigmoid),
          r=("drow",), w=("drow",))
    B.act(lambda E: E.activation(lnrow[0:1, :], drow[0:1, :], AF.Ln),
          r=("drow",), w=("lnrow",))
    B.vop(lambda E: E.tensor_single_scalar(tri01[:, :], iotaE[:, :], 0.0,
                                           OP.is_ge), r=("iotaE",), w=("tri01",))
    for h in range(NH):
        B.mm([(B.ones_row_f[:, :], lnrow[0:1, h:h + 1], psCar[0][:, 0:1],
               True, True)], r=("lnrow", "ones_row_f"), w=("psCar0",))
        B.vop(lambda E: E.tensor_copy(lnd_col[:, :], psCar[0][:, 0:1]),
              r=("psCar0",), w=("lnd_col",))
        B.vop(lambda E: E.tensor_scalar_mul(uscr[:, 0:8], sc_ud[:, :],
                                            lnd_col[:, 0:1]),
              r=("sc_ud", "lnd_col"), w=("uscr",))
        B.act(lambda E, h=h: E.activation(udb[:, h * 8:(h + 1) * 8],
                                          uscr[:, 0:8], AF.Exp),
              r=("uscr",), w=("udb",))
    for ki, (srck, src) in enumerate((("kv_bf", kv_bf), ("k_phi", k_phi))):
        for h in range(NH):
            gp = []
            for cp in range(8):
                gp.append((udb[:, h * 8 + (7 - cp):h * 8 + (8 - cp)],
                           src[:, cp * D + h * 128: cp * D + (h + 1) * 128],
                           psCar[0][0:1, 0:128], cp == 0, cp == 7))
            B.mm(gp, r=(srck, "udb"), w=("psCar0",), tag=f"gc{ki}_{h}")
            B.vop(lambda E, ki=ki, h=h: E.tensor_copy(
                carry_all[0:1, ki * 2048 + h * 128: ki * 2048 + (h + 1) * 128],
                psCar[0][0:1, 0:128]), r=("psCar0",), w=("carry_all",))
    B.dma("y", B.dram["carry_out"][0:1, :], carry_all[0:1, :],
          r=("carry_all",), w=("carry_o",))
    B.dma("y", kphi_d[:, :], k_phi[:, :], r=("k_phi",), w=("kphi_o",))
    B.dma("y", kv_d[:, :], kv_bf[:, :], r=("kv_bf",), w=("kv_o",))
    S.barrier()
    stS.close()
    st.close()
    return


def phase_l1f(B):
    S = B.S
    st = B.ctx.enter_context(ExitStack())
    B.phase_stack = st
    B.emit_consts(st)

    x_b = B.dint("x_bounce", [D, T_CORE])
    hsum_d = B.dint("hsum_d", [I1, T_CORE], BF16)
    out_d = B.dout("x_final", [T_CORE, D])
    bg_d = B.din("bgate", [D, I1])
    bu_d = B.din("bup", [D, I1])
    bd_d = B.din("bdown", [I1, D])
    gA_d = B.din("gateA", [NE, D, RLORA])
    gB_d = B.din("gateB", [NE, RLORA, I1])
    uA_d = B.din("upA", [NE, D, RLORA])
    uB_d = B.din("upB", [NE, RLORA, I1])
    dA_d = B.din("downA", [NE, I1, RLORA])
    dB_d = B.din("downB", [NE, RLORA, D])

    def xslice(hc):
        return bass.AP(x_b, hc * 128 * 1024, [[1024, 128], [1, 1024]])

    xn3 = B.sb("xn3", [128, HC * 1024], BF16, st)
    we_all = B.sb("we_all", [128, 8 * NE], F32, st)
    we_rows = B.sb("we_rows", [1, NE * 1024], F32, st)
    xat = B.sb("xat", [64, 2 * NE * 1024], BF16, st)
    hA_w = B.sb("hA_w", [64, NE * 1024], BF16, st)

    stN = ExitStack()
    xsl = [B.sb(f"xslN{i}", [128, 1024], F32, stN) for i in range(2)]
    xsq = B.sb("xsqN", [128, 1024], F32, stN)
    tmpN = B.sb("tmpN", [128, 1024], F32, stN)
    mrow = B.sb("mrow", [1, 1024], F32, stN)
    rrow = B.sb("rrow", [1, 1024], F32, stN)
    csrow = B.sb("csrow", [1, NE], F32, stN)
    confw = B.sb("confw", [128, HC * NE], F32, stN)
    xw_tm = B.sb("xw_tm", [128, 8 * NE], F32, stN)
    rc_tm = B.sb("rc_tm", [128, 8], F32, stN)
    mrc_tm = B.sb("mrc_tm", [128, 8], F32, stN)
    csb_b = B.sb("csb_b", [128, NE], F32, stN)
    cb_b = B.sb("cb_b", [128, NE], F32, stN)
    we_b = B.sb("we_b", [128, NE], F32, stN)
    t8a = B.sb("t8a", [128, NE], F32, stN)
    t8b = B.sb("t8b", [128, NE], F32, stN)
    t8c = B.sb("t8c", [128, NE], F32, stN)
    c1 = B.sb("c1", [128, 1], F32, stN)
    c2 = B.sb("c2", [128, 1], F32, stN)
    psSu = [B.ps(f"psSu{i}", [128, 512], F32, stN) for i in range(2)]
    psSq = [B.ps(f"psSq{i}", [128, 512], F32, stN) for i in range(2)]
    psMu = B.ps("psMu", [128, 1024], F32, stN)
    psRr = B.ps("psRr", [128, 1024], F32, stN)
    B.psbc_name = "psSu0"

    confw_d = B.din("confWT", [D, NE])
    for hc in range(HC):
        B.dma("y", confw[:, hc * NE:(hc + 1) * NE],
              confw_d[hc * 128:(hc + 1) * 128, :], r=(), w=("confw",))
    wealth_r = B.load_row(stN, "wealth_r", "wealth", NE)
    confb_r = B.load_row(stN, "confb_r", "confb", NE)

    for hc in range(HC):
        hb = hc % 2
        xk = f"xslN{hb}"
        B.dma("y", xsl[hb][:, :], xslice(hc), r=("x_b",), w=(xk,))
        B.act(lambda E, hb=hb: E.activation(xsq[:, :], xsl[hb][:, :],
                                            AF.Square), r=(xk,), w=("xsqN",))
        for j in range(2):
            B.mm([(B.ones_col_f[:, :], xsl[hb][:, j * 512:(j + 1) * 512],
                   psSu[j][0:1, 0:512], hc == 0, hc == HC - 1)],
                 r=(xk, "ones_col_f"), w=(f"psSu{j}",))
            B.mm([(B.ones_col_f[:, :], xsq[:, j * 512:(j + 1) * 512],
                   psSq[j][0:1, 0:512], hc == 0, hc == HC - 1)],
                 r=("xsqN", "ones_col_f"), w=(f"psSq{j}",))
        for tc in range(8):
            B.mm([(xsl[hb][:, tc * 128:(tc + 1) * 128],
                   confw[:, hc * NE:(hc + 1) * NE],
                   psMu[:, tc * NE:(tc + 1) * NE], hc == 0, hc == HC - 1)],
                 r=(xk, "confw"), w=("psMu",))
        B.mm([(B.ones_col_f[:, :], confw[:, hc * NE:(hc + 1) * NE],
               psRr[0:1, 0:NE], hc == 0, hc == HC - 1)],
             r=("confw", "ones_col_f"), w=("psRr",))
    B.vop(lambda E: E.tensor_copy(xw_tm[:, :], psMu[:, 0:8 * NE]),
          r=("psMu",), w=("xw_tm",))
    B.vop(lambda E: E.tensor_copy(csrow[0:1, :], psRr[0:1, 0:NE]),
          r=("psRr",), w=("csrow",))
    for row, rk, dst, dk in ((csrow, "csrow", csb_b, "csb_b"),
                             (confb_r, "confb_r", cb_b, "cb_b"),
                             (wealth_r, "wealth_r", we_b, "we_b")):
        B.mm([(B.ones_row_f[:, :], row[0:1, 0:NE], psMu[:, 0:NE], True, True)],
             r=(rk, "ones_row_f"), w=("psMu",))
        B.vop(lambda E, dst=dst: E.tensor_copy(dst[:, :], psMu[:, 0:NE]),
              r=("psMu",), w=(dk,))
    for j in range(2):
        B.vop(lambda E, j=j: E.tensor_single_scalar(
            mrow[0:1, j * 512:(j + 1) * 512], psSu[j][0:1, 0:512],
            1.0 / D, OP.mult), r=(f"psSu{j}",), w=("mrow",))
        B.vop(lambda E, j=j: E.tensor_single_scalar(
            rrow[0:1, j * 512:(j + 1) * 512], psSq[j][0:1, 0:512],
            1.0 / D, OP.mult), r=(f"psSq{j}",), w=("rrow",))
    B.vop(lambda E: E.tensor_mul(tmpN[0:1, 0:1024], mrow[0:1, :], mrow[0:1, :]),
          r=("mrow",), w=("tmpN",))
    B.vop(lambda E: E.scalar_tensor_tensor(
        rrow[0:1, :], tmpN[0:1, 0:1024], -1.0, rrow[0:1, :], OP.mult, OP.add),
        r=("tmpN", "rrow"), w=("rrow",))
    B.vop(lambda E: E.tensor_single_scalar(rrow[0:1, :], rrow[0:1, :], EPS,
                                           OP.add), r=("rrow",), w=("rrow",))
    B.act(lambda E: E.activation(rrow[0:1, :], rrow[0:1, :], AF.Sqrt),
          r=("rrow",), w=("rrow",))
    B.vop(lambda E: E.reciprocal(rrow[0:1, :], rrow[0:1, :]),
          r=("rrow",), w=("rrow",))
    B.vop(lambda E: E.tensor_mul(tmpN[0:1, 0:1024], mrow[0:1, :], rrow[0:1, :]),
          r=("mrow", "rrow"), w=("tmpN",))
    for tc in range(8):
        B.row_to_col(rrow[0:1, tc * 128:(tc + 1) * 128], 128,
                     rc_tm[:, tc:tc + 1], "rrow", "rc_tm")
        B.row_to_col(tmpN[0:1, tc * 128:(tc + 1) * 128], 128,
                     mrc_tm[:, tc:tc + 1], "tmpN", "mrc_tm")
    B.vop(lambda E: E.tensor_single_scalar(mrc_tm[:, :], mrc_tm[:, :], -1.0,
                                           OP.mult), r=("mrc_tm",), w=("mrc_tm",))
    for j in range(2):
        B.mm([(B.ones_row_f[:, :], mrow[0:1, j * 512:(j + 1) * 512],
               psMu[:, j * 512:(j + 1) * 512], True, True)],
             r=("mrow", "ones_row_f"), w=("psMu",))
        B.mm([(B.ones_row_f[:, :], rrow[0:1, j * 512:(j + 1) * 512],
               psRr[:, j * 512:(j + 1) * 512], True, True)],
             r=("rrow", "ones_row_f"), w=("psRr",))
    for hc in range(HC):
        hb = hc % 2
        xk = f"xslN{hb}"
        B.dma("y", xsl[hb][:, :], xslice(hc), r=("x_b",), w=(xk,))
        B.vop(lambda E, hb=hb: E.tensor_sub(
            tmpN[:, :], xsl[hb][:, :], psMu[:, 0:1024]),
            r=(xk, "psMu"), w=("tmpN",))
        B.vop(lambda E, hc=hc: E.tensor_tensor(
            xn3[:, hc * 1024:(hc + 1) * 1024], tmpN[:, :], psRr[:, 0:1024],
            OP.mult), r=("tmpN", "psRr"), w=("xn3",))
    for tc in range(8):
        B.vop(lambda E, tc=tc: E.tensor_scalar_mul(
            t8a[:, :], xw_tm[:, tc * NE:(tc + 1) * NE], rc_tm[:, tc:tc + 1]),
            r=("xw_tm", "rc_tm"), w=("t8a",))
        B.vop(lambda E, tc=tc: E.scalar_tensor_tensor(
            t8a[:, :], csb_b[:, :], mrc_tm[:, tc:tc + 1], t8a[:, :],
            OP.mult, OP.add), r=("csb_b", "mrc_tm", "t8a"), w=("t8a",))
        B.vop(lambda E: E.tensor_tensor(t8a[:, :], t8a[:, :], cb_b[:, :],
                                        OP.add), r=("t8a", "cb_b"), w=("t8a",))
        B.act(lambda E: E.activation(t8a[:, :], t8a[:, :], AF.Sigmoid),
              r=("t8a",), w=("t8a",))
        B.vop(lambda E: E.tensor_mul(t8a[:, :], t8a[:, :], we_b[:, :]),
              r=("t8a", "we_b"), w=("t8a",))
        B.vop(lambda E: E.reduce_max(c1[:, :], t8a[:, :], axis=AX.X),
              r=("t8a",), w=("c1",))
        B.vop(lambda E: E.tensor_scalar(t8b[:, :], t8a[:, :], c1[:, 0:1],
                                        NEG_BIG, OP.is_ge, OP.mult),
              r=("t8a", "c1"), w=("t8b",))
        B.vop(lambda E: E.tensor_tensor(t8b[:, :], t8a[:, :], t8b[:, :],
                                        OP.add), r=("t8a", "t8b"), w=("t8b",))
        B.vop(lambda E: E.reduce_max(c2[:, :], t8b[:, :], axis=AX.X),
              r=("t8b",), w=("c2",))
        B.vop(lambda E: E.tensor_scalar_sub(t8c[:, :], t8a[:, :], c1[:, 0:1]),
              r=("t8a", "c1"), w=("t8c",))
        B.act(lambda E: E.activation(t8c[:, :], t8c[:, :], AF.Exp),
              r=("t8c",), w=("t8c",))
        B.vop(lambda E: E.tensor_scalar(t8b[:, :], t8a[:, :], c2[:, 0:1], 1.0,
                                        OP.is_ge, OP.mult),
              r=("t8a", "c2"), w=("t8b",))
        B.vop(lambda E: E.tensor_sub(c2[:, :], c2[:, :], c1[:, :]),
              r=("c1", "c2"), w=("c2",))
        B.act(lambda E: E.activation(c2[:, :], c2[:, :], AF.Exp),
              r=("c2",), w=("c2",))
        B.vop(lambda E: E.tensor_single_scalar(c2[:, :], c2[:, :], 1.0,
                                               OP.add), r=("c2",), w=("c2",))
        B.vop(lambda E: E.reciprocal(c2[:, :], c2[:, :]), r=("c2",), w=("c2",))
        B.vop(lambda E: E.tensor_mul(t8c[:, :], t8c[:, :], t8b[:, :]),
              r=("t8c", "t8b"), w=("t8c",))
        B.vop(lambda E, tc=tc: E.tensor_scalar_mul(
            we_all[:, tc * NE:(tc + 1) * NE], t8c[:, :], c2[:, 0:1]),
            r=("t8c", "c2"), w=("we_all",))
    for tc in range(8):
        for e in range(NE):
            B.trans(psSu[0][0:1, 0:128],
                    we_all[:, tc * NE + e: tc * NE + e + 1],
                    B.ident_f[:, :], r=("we_all", "ident_f"), w=("psSu0",))
            B.vop(lambda E, tc=tc, e=e: E.tensor_copy(
                we_rows[0:1, e * 1024 + tc * 128: e * 1024 + (tc + 1) * 128],
                psSu[0][0:1, 0:128]), r=("psSu0",), w=("we_rows",))
    S.barrier()
    stN.close()

    stX = ExitStack()
    psXA = B.ps("psXA", [128, 1024], F32, stX)
    gasb = B.sb("gasb", [128, RLORA], BF16, stX)
    for li, ad in enumerate((gA_d, uA_d)):
        for e in range(NE):
            for hc in range(HC):
                B.dma("g", gasb[:, :], ad[e, hc * 128:(hc + 1) * 128, :],
                      r=(), w=("gasb",))
                B.mm([(gasb[:, :],
                       xn3[:, hc * 1024 + j * 512: hc * 1024 + (j + 1) * 512],
                       psXA[0:64, j * 512:(j + 1) * 512], hc == 0, hc == HC - 1)
                      for j in range(2)],
                     r=("gasb", "xn3"), w=("psXA",), tag=f"xa{li}_{e}_{hc}")
            B.vop(lambda E, li=li, e=e: E.tensor_single_scalar(
                xat[0:64, li * NE * 1024 + e * 1024:
                    li * NE * 1024 + (e + 1) * 1024],
                psXA[0:64, 0:1024], SCALING, OP.mult),
                r=("psXA",), w=("xat",))
    S.barrier()
    stX.close()

    stI = ExitStack()
    bg_sb = [B.sb(f"bg_sb{i}", [128, HC * 128], BF16, stI) for i in range(2)]
    bu_sb = [B.sb(f"bu_sb{i}", [128, HC * 128], BF16, stI) for i in range(2)]
    gB_sb = B.sb("gB_sb", [64, NE * 128], BF16, stI)
    uB_sb = B.sb("uB_sb", [64, NE * 128], BF16, stI)
    gBn_sb = B.sb("gBn_sb", [64, NE * 128], BF16, stI)
    uBn_sb = B.sb("uBn_sb", [64, NE * 128], BF16, stI)
    dA_sb = B.sb("dA_sb", [128, NE * RLORA], BF16, stI)
    sg_sb = B.sb("sg_sb", [128, 1024], F32, stI)
    he_sb = B.sb("he_sb", [128, 1024], BF16, stI)
    hw_sb = B.sb("hw_sb", [128, 1024], F32, stI)
    hs_sb = B.sb("hs_sb", [128, 1024], F32, stI)
    hsb_bf = B.sb("hsb_bf", [128, 1024], BF16, stI)
    weU = B.sb("weU", [128, NE * 1024], BF16, stI)
    hA_sb = B.sb("hA_sb", [64, NE * 1024], F32, stI)
    psG = B.ps("psGf", [128, 1024], F32, stI)
    psU = B.ps("psUf", [128, 1024], F32, stI)
    psH = B.ps("psHf", [128, 1024], F32, stI)
    for e in range(NE):
        for j in range(2):
            B.mm([(B.ones_row_f[:, :],
                   we_rows[0:1, e * 1024 + j * 512: e * 1024 + (j + 1) * 512],
                   psG[:, j * 512:(j + 1) * 512], True, True)],
                 r=("we_rows", "ones_row_f"), w=("psGf",))
        B.vop(lambda E, e=e: E.tensor_copy(weU[:, e * 1024:(e + 1) * 1024],
                                           psG[:, 0:1024]),
              r=("psGf",), w=("weU",))
    B.vop(lambda E: E.memset(hA_sb[:, :], 0.0), r=(), w=("hA_sb",))

    NIC = I1 // 128
    for ic in range(NIC):
        cb = ic % 2
        bgk, buk = f"bg_sb{cb}", f"bu_sb{cb}"
        for hc in range(HC):
            B.dma("g", bg_sb[cb][:, hc * 128:(hc + 1) * 128],
                  bg_d[hc * 128:(hc + 1) * 128, ic * 128:(ic + 1) * 128],
                  r=(), w=(bgk,))
            B.dma("g", bu_sb[cb][:, hc * 128:(hc + 1) * 128],
                  bu_d[hc * 128:(hc + 1) * 128, ic * 128:(ic + 1) * 128],
                  r=(), w=(buk,))
        B.dma("g", gB_sb[0:64, :],
              bass.AP(gB_d, ic * 128, [[I1, 64], [RLORA * I1, NE], [1, 128]]),
              r=(), w=("gB_sb",))
        B.dma("g", uB_sb[0:64, :],
              bass.AP(uB_d, ic * 128, [[I1, 64], [RLORA * I1, NE], [1, 128]]),
              r=(), w=("uB_sb",))
        B.dma("g", dA_sb[:, :],
              bass.AP(dA_d, ic * 128 * RLORA,
                      [[RLORA, 128], [I1 * RLORA, NE], [1, RLORA]]),
              r=(), w=("dA_sb",))
        B.vop(lambda E: E.tensor_single_scalar(gBn_sb[0:64, :], gB_sb[0:64, :],
                                               -1.0, OP.mult),
              r=("gB_sb",), w=("gBn_sb",))
        B.vop(lambda E: E.tensor_single_scalar(uBn_sb[0:64, :], uB_sb[0:64, :],
                                               -1.0, OP.mult),
              r=("uB_sb",), w=("uBn_sb",))
        gp, up = [], []
        for hc in range(HC):
            for j in range(2):
                sl = slice(j * 512, (j + 1) * 512)
                gp.append((bg_sb[cb][:, hc * 128:(hc + 1) * 128],
                           xn3[:, hc * 1024 + j * 512: hc * 1024 + (j + 1) * 512],
                           psG[:, sl], hc == 0, False))
                up.append((bu_sb[cb][:, hc * 128:(hc + 1) * 128],
                           xn3[:, hc * 1024 + j * 512: hc * 1024 + (j + 1) * 512],
                           psU[:, sl], hc == 0, False))
        B.mm(gp, r=(bgk, "xn3"), w=("psGf",), tag=f"bg{ic}")
        B.mm(up, r=(buk, "xn3"), w=("psUf",), tag=f"bu{ic}")
        B.vop(lambda E: E.memset(hs_sb[:, :], 0.0), r=(), w=("hs_sb",))
        for e in range(NE):
            gp = []
            for j in range(2):
                sl = slice(j * 512, (j + 1) * 512)
                gp.append((gB_sb[0:64, e * 128:(e + 1) * 128],
                           xat[0:64, e * 1024 + j * 512: e * 1024 + j * 512 + 512],
                           psG[:, sl], False, e == NE - 1))
                if e > 0:
                    gp.append((gBn_sb[0:64, (e - 1) * 128: e * 128],
                               xat[0:64, (e - 1) * 1024 + j * 512:
                                   (e - 1) * 1024 + j * 512 + 512],
                               psG[:, sl], False, False))
            B.mm(gp, r=("gB_sb", "gBn_sb", "xat"), w=("psGf",), tag=f"lg{ic}_{e}")
            up = []
            off = NE * 1024
            for j in range(2):
                sl = slice(j * 512, (j + 1) * 512)
                up.append((uB_sb[0:64, e * 128:(e + 1) * 128],
                           xat[0:64, off + e * 1024 + j * 512:
                               off + e * 1024 + j * 512 + 512],
                           psU[:, sl], False, e == NE - 1))
                if e > 0:
                    up.append((uBn_sb[0:64, (e - 1) * 128: e * 128],
                               xat[0:64, off + (e - 1) * 1024 + j * 512:
                                   off + (e - 1) * 1024 + j * 512 + 512],
                               psU[:, sl], False, False))
            B.mm(up, r=("uB_sb", "uBn_sb", "xat"), w=("psUf",), tag=f"lu{ic}_{e}")
            B.act(lambda E: E.activation(sg_sb[:, :], psG[:, 0:1024], AF.Silu),
                  r=("psGf",), w=("sg_sb",))
            B.vop(lambda E: E.tensor_tensor(he_sb[:, :], sg_sb[:, :],
                                            psU[:, 0:1024], OP.mult),
                  r=("sg_sb", "psUf"), w=("he_sb",))
            hp = []
            for j in range(2):
                sl = slice(j * 512, (j + 1) * 512)
                hp.append((dA_sb[:, e * RLORA:(e + 1) * RLORA],
                           he_sb[:, sl], psH[0:64, sl], True, True))
            B.mm(hp, r=("dA_sb", "he_sb"), w=("psHf",), tag=f"ha{ic}_{e}")
            B.vop(lambda E, e=e: E.scalar_tensor_tensor(
                hA_sb[0:64, e * 1024:(e + 1) * 1024], psH[0:64, 0:1024], 1.0,
                hA_sb[0:64, e * 1024:(e + 1) * 1024], OP.mult, OP.add),
                r=("psHf", "hA_sb"), w=("hA_sb",))
            B.vop(lambda E, e=e: E.tensor_tensor(
                hw_sb[:, :], he_sb[:, :], weU[:, e * 1024:(e + 1) * 1024],
                OP.mult), r=("he_sb", "weU"), w=("hw_sb",), eng="g")
            B.vop(lambda E: E.tensor_tensor(hs_sb[:, :], hs_sb[:, :],
                                            hw_sb[:, :], OP.add),
                  r=("hs_sb", "hw_sb"), w=("hs_sb",))
        B.vop(lambda E: E.tensor_copy(hsb_bf[:, :], hs_sb[:, :]),
              r=("hs_sb",), w=("hsb_bf",))
        B.dma("y", hsum_d[ic * 128:(ic + 1) * 128, :], hsb_bf[:, :],
              r=("hsb_bf",), w=("hsum_d",), tag=f"hs{ic}")
    for e in range(NE):
        B.vop(lambda E, e=e: E.scalar_tensor_tensor(
            hA_w[0:64, e * 1024:(e + 1) * 1024],
            hA_sb[0:64, e * 1024:(e + 1) * 1024], SCALING,
            weU[0:64, e * 1024:(e + 1) * 1024], OP.mult, OP.mult),
            r=("hA_sb", "weU"), w=("hA_w",))
    S.barrier()
    stI.close()

    stD = ExitStack()
    hss = B.sb("hss", [128, 1024], BF16, stD)
    bd_sb = [B.sb(f"bd_sb{i}", [128, NIC * 128], BF16, stD) for i in range(2)]
    dB_sb = B.sb("dB_sb", [64, NE * 128], BF16, stD)
    xslD = [B.sb(f"xslD{i}", [128, 1024], F32, stD) for i in range(2)]
    ost = B.sb("ost", [128, 8 * 128], F32, stD)
    psO = [B.ps(f"psOf{i}", [128, 1024], F32, stD) for i in range(2)]
    psTr = B.ps("psTr", [128, 512], F32, stD)
    for oc in range(16):
        ob = oc % 2
        bdk, xk = f"bd_sb{ob}", f"xslD{ob}"
        pok = f"psOf{ob}"
        for ic in range(NIC):
            B.dma("g", bd_sb[ob][:, ic * 128:(ic + 1) * 128],
                  bd_d[ic * 128:(ic + 1) * 128, oc * 128:(oc + 1) * 128],
                  r=(), w=(bdk,))
        B.dma("g", dB_sb[0:64, :],
              bass.AP(dB_d, oc * 128, [[D, 64], [RLORA * D, NE], [1, 128]]),
              r=(), w=("dB_sb",))
        B.dma("y", xslD[ob][:, :], xslice(oc), r=("x_b",), w=(xk,))
        for ic in range(NIC):
            B.dma("y", hss[:, :], hsum_d[ic * 128:(ic + 1) * 128, :],
                  r=("hsum_d",), w=("hss",), tag=f"hl{oc}_{ic}")
            B.mm([(bd_sb[ob][:, ic * 128:(ic + 1) * 128],
                   hss[:, j * 512:(j + 1) * 512],
                   psO[ob][:, j * 512:(j + 1) * 512], ic == 0, False)
                  for j in range(2)],
                 r=(bdk, "hss"), w=(pok,), tag=f"dn{oc}_{ic}")
        lp = []
        for e in range(NE):
            for j in range(2):
                sl = slice(j * 512, (j + 1) * 512)
                lp.append((dB_sb[0:64, e * 128:(e + 1) * 128],
                           hA_w[0:64, e * 1024 + j * 512: e * 1024 + j * 512 + 512],
                           psO[ob][:, sl], False, e == NE - 1))
        B.mm(lp, r=("dB_sb", "hA_w"), w=(pok,), tag=f"dl{oc}")
        B.vop(lambda E, ob=ob: E.tensor_tensor(
            xslD[ob][:, :], xslD[ob][:, :], psO[ob][:, 0:1024], OP.add),
            r=(pok, xk), w=(xk,))
        for t4 in range(2):
            for ti in range(4):
                tc = t4 * 4 + ti
                B.trans(psTr[:, ti * 128:(ti + 1) * 128],
                        xslD[ob][:, tc * 128:(tc + 1) * 128], B.ident_f[:, :],
                        r=(xk, "ident_f"), w=("psTr",))
            B.vop(lambda E, t4=t4: E.tensor_copy(
                ost[:, t4 * 512:(t4 + 1) * 512], psTr[:, 0:512]),
                r=("psTr",), w=("ost",))
        B.dma("y", bass.AP(out_d, oc * 128,
                           [[D, 128], [128 * D, 8], [1, 128]]),
              ost[:, :], r=("ost",), w=("out_d",), tag=f"of{oc}")
    S.barrier()
    stD.close()
    st.close()


PHASE_FNS = {}


def build_program(phases, first, last):
    B = Builder(phases, first, last)
    with B.ctx:
        if not first:
            xin = B.din("x_in_fm", [D, T_CORE])
            xb = B.dint("x_bounce", [D, T_CORE])
            B.dma("y", xb[:, :], xin[:, :], r=(), w=("x_b",))
            B.S.barrier()
        for ph in phases:
            PHASE_FNS[ph](B)
        if not last:
            xb = B.dint("x_bounce", [D, T_CORE])
            xout = B.dout("x_out_fm", [D, T_CORE])
            B.dma("y", xout[:, :], xb[:, :], r=("x_b",), w=("x_out",))
            B.S.barrier()
        sems = {}
        sstack = B.ctx.enter_context(ExitStack())
        for e in ENG_ATTR:
            sems[e] = sstack.enter_context(B.nc.semaphore(f"sem_{e}"))
        B.S.wait_all()
        B.S.emit(sems)
    return B


_CACHE = {}


def _get_program(key, phases, first, last):
    if key not in _CACHE:
        _CACHE[key] = build_program(phases, first, last)
    return _CACHE[key]


PHASE_FNS["l0a"] = phase_l0a
PHASE_FNS["l0f"] = phase_l0f
PHASE_FNS["l1a_pre"] = lambda B: phase_l1a(B, "pre")
PHASE_FNS["l1a_main"] = lambda B: phase_l1a(B, "main")
PHASE_FNS["l1f"] = phase_l1f

CHAIN = [["l0a"], ["l0f"], ["l1a_pre"], ["l1a_main"], ["l1f"]]


def kernel(**inputs):
    sh, cores = host_prepare(inputs)
    xcur = None
    for pi, phases in enumerate(CHAIN):
        first = pi == 0
        last = (pi == len(CHAIN) - 1) and ("l1f" in phases)
        B = _get_program("+".join(phases), phases, first, last)
        in_maps = []
        for c in range(8):
            m = {}
            for n in B.in_names:
                if n == "x_in_fm":
                    m["in_x_in_fm"] = xcur[c]
                elif n in ("q_io", "kphi_io", "kv_io"):
                    m["in_" + n] = aux["out_" + n][c]
                elif n == "carry_in":
                    src_c = c - 1 if c % 2 == 1 else None
                    m["in_carry_in"] = (aux["out_carry_out"][src_c]
                                        if src_c is not None else
                                        np.zeros((1, 4096), np.float32))
                else:
                    m["in_" + n] = np.asarray(sh[n] if n in sh else cores[c][n])
            in_maps.append(m)
        res = run_bass_kernel_spmd(B.nc, in_maps, core_ids=list(range(8)))
        aux = {}
        for key in ("out_q_io", "out_kphi_io", "out_kv_io", "out_carry_out"):
            if key in res.results[0]:
                aux[key] = [res.results[c][key] for c in range(8)]
        if not last:
            xcur = [res.results[c]["out_x_out_fm"] for c in range(8)]
    full = np.zeros((4, T_SEQ, D), dtype=np.float32)
    for c in range(8):
        b, half = c // 2, c % 2
        if last:
            full[b, half * T_CORE:(half + 1) * T_CORE, :] = \
                res.results[c]["out_x_final"]
        else:
            full[b, half * T_CORE:(half + 1) * T_CORE, :] = \
                res.results[c]["out_x_out_fm"].T
    return full
